# version 10 (final)
# speedup vs baseline: 1.5596x; 1.4272x over previous
import math
from contextlib import ExitStack

import numpy as np
import concourse.bass as bass
import concourse.mybir as mybir
from concourse.bass_utils import run_bass_kernel_spmd

F32 = mybir.dt.float32
BF16 = mybir.dt.bfloat16
AF = mybir.ActivationFunctionType
ALU = mybir.AluOpType
AX = mybir.AxisListType

NCORES = 8
D = 1024
NDM = 8


def ml_bf16():
    import ml_dtypes
    return ml_dtypes.bfloat16


class BufState:
    __slots__ = ("writer", "readers")

    def __init__(self):
        self.writer = None
        self.readers = []


class Buf:
    __slots__ = ("ap_", "st", "name")

    def __init__(self, ap, name="", st=None):
        self.ap_ = ap
        self.name = name
        self.st = st if st is not None else BufState()

    def view(self, ap, name=None):
        return Buf(ap, name or self.name, self.st)

    @property
    def writer(self):
        return self.st.writer

    @writer.setter
    def writer(self, v):
        self.st.writer = v

    @property
    def readers(self):
        return self.st.readers

    @readers.setter
    def readers(self, v):
        self.st.readers = v

    def __getitem__(self, idx):
        return self.ap_[idx]

    @property
    def ap(self):
        return self.ap_


class Prog:
    ENGINES = ("sync", "tensor", "vector", "scalar", "gpsimd")

    def __init__(self, nc):
        self.nc = nc
        self.es = ExitStack()
        self.q = {e: [] for e in self.ENGINES}
        self.sems = {}
        for e in self.ENGINES:
            self.sems[(e, "c")] = Sem(self.es.enter_context(nc.semaphore(f"s_{e}_c")))
            self.sems[(e, "d")] = Sem(self.es.enter_context(nc.semaphore(f"s_{e}_d")))
            self.sems[(e, "d2")] = Sem(self.es.enter_context(nc.semaphore(f"s_{e}_d2")))
        self.dma_lane = {e: 0 for e in self.ENGINES}
        self.hw = {e: {} for e in self.ENGINES}
        self.ninst = 0
        self.input_names = []
        self.ccsems = []

    def sb(self, name, shape, dt=F32):
        h = self.es.enter_context(self.nc.sbuf_tensor(name, shape, dt))
        return Buf(h.ap() if hasattr(h, "ap") else h, name)

    def psum(self, name, shape, dt=F32):
        h = self.es.enter_context(self.nc.psum_tensor(name, shape, dt))
        return Buf(h.ap() if hasattr(h, "ap") else h, name)

    def dram(self, name, shape, dt=F32, kind=None, addr_space=None):
        kw = {}
        if kind:
            kw["kind"] = kind
        if addr_space:
            kw["addr_space"] = addr_space
        t = self.nc.dram_tensor(name, shape, dt, **kw)
        if kind == "ExternalInput":
            self.input_names.append(name)
        return Buf(t.ap(), name)

    def emit(self, eng, op, reads=(), writes=(), dma=False, inck=None):
        if dma:
            lane = self.dma_lane[eng]
            self.dma_lane[eng] ^= 1
            sem = self.sems[(eng, "d" if lane == 0 else "d2")]
        else:
            sem = self.sems[(eng, "c")]
        waits = {}

        def need(mark):
            if mark is None:
                return
            weng, wsem, wcount, wdma = mark
            if weng == "tensor" and eng == "tensor" and not wdma:
                return
            prev = waits.get(wsem)
            if prev is None or wcount > prev:
                waits[wsem] = wcount

        for b in reads:
            need(b.writer)
        for b in writes:
            need(b.writer)
            for r in b.readers:
                need(r)
        if dma and sem.n > 0:
            prev = waits.get(sem)
            if prev is None or sem.n > prev:
                waits[sem] = sem.n
        hww = self.hw[eng]
        wv = []
        for s, n in waits.items():
            if hww.get(s, -1) < n:
                wv.append((s.h, n))
                hww[s] = n
        k = 16 if dma else 1
        if inck is not None:
            k = inck
        sem.inc(k)
        mark = (eng, sem, sem.n, dma)

        def f(e, wv=tuple(wv), op=op, h=sem.h, k=k):
            for (sh, n) in wv:
                e.wait_ge(sh, n)
            op(e).then_inc(h, k)

        self.q[eng].append(f)
        self.ninst += 1
        for b in writes:
            b.writer = mark
            b.readers = []
        for b in reads:
            b.readers.append(mark)
        return mark

    def emit_group(self, eng, ops, reads=(), writes=()):
        sem = self.sems[(eng, "c")]
        waits = {}

        def need(mark):
            if mark is None:
                return
            weng, wsem, wcount, wdma = mark
            if weng == "tensor" and eng == "tensor" and not wdma:
                return
            prev = waits.get(wsem)
            if prev is None or wcount > prev:
                waits[wsem] = wcount

        for b in reads:
            need(b.writer)
        for b in writes:
            need(b.writer)
            for r in b.readers:
                need(r)
        hww = self.hw[eng]
        wv = []
        for s, n in waits.items():
            if hww.get(s, -1) < n:
                wv.append((s.h, n))
                hww[s] = n
        sem.inc(1)
        mark = (eng, sem, sem.n, False)
        last = len(ops) - 1

        for k, op in enumerate(ops):
            if k == 0 and k == last:
                def f(e, wv=tuple(wv), op=op, h=sem.h):
                    for (sh, n) in wv:
                        e.wait_ge(sh, n)
                    op(e).then_inc(h, 1)
            elif k == 0:
                def f(e, wv=tuple(wv), op=op):
                    for (sh, n) in wv:
                        e.wait_ge(sh, n)
                    op(e)
            elif k == last:
                def f(e, op=op, h=sem.h):
                    op(e).then_inc(h, 1)
            else:
                def f(e, op=op):
                    op(e)
            self.q[eng].append(f)
            self.ninst += 1
        for b in writes:
            b.writer = mark
            b.readers = []
        for b in reads:
            b.readers.append(mark)
        return mark

    def emit_cc(self, op, reads=(), writes=()):
        eng = "gpsimd"
        name = f"cc{len(self.ccsems)}"
        ccs = Sem(self.es.enter_context(self.nc.semaphore(name)))
        self.ccsems.append(ccs)
        waits = {}

        def need(mark):
            if mark is None:
                return
            weng, wsem, wcount, wdma = mark
            prev = waits.get(wsem)
            if prev is None or wcount > prev:
                waits[wsem] = wcount

        for b in reads:
            need(b.writer)
        for b in writes:
            need(b.writer)
            for r in b.readers:
                need(r)
        hww = self.hw[eng]
        wv = []
        for s, n in waits.items():
            if hww.get(s, -1) < n:
                wv.append((s.h, n))
                hww[s] = n
        ccs.inc(1)
        mark = (eng, ccs, 1, True)

        def f(e, wv=tuple(wv), op=op, h=ccs.h):
            for (sh, n) in wv:
                e.wait_ge(sh, n)
            op(e).then_inc(h, 1)

        self.q[eng].append(f)
        self.ninst += 1
        for b in writes:
            b.writer = mark
            b.readers = []
        for b in reads:
            b.readers.append(mark)
        return mark

    def barrier_on(self, eng, marks):
        hww = self.hw[eng]
        wv = []
        for (weng, wsem, wcount, wdma) in marks:
            if hww.get(wsem, -1) < wcount:
                wv.append((wsem.h, wcount))
                hww[wsem] = wcount
        if wv:
            self.q[eng].append(lambda e, wv=tuple(wv): [e.wait_ge(sh, n) for (sh, n) in wv])

    def finish(self, final_eng="sync"):
        wv = [(s.h, s.n) for s in list(self.sems.values()) + self.ccsems if s.n > 0]
        self.q[final_eng].append(lambda e, wv=tuple(wv): [e.wait_ge(h, n) for (h, n) in wv])
        nc = self.nc
        with self.es, nc.Block() as block:
            for e in self.ENGINES:
                fns = self.q[e]
                getattr(block, e)(lambda eng, fns=fns: [fn(eng) for fn in fns])
        nc.input_names = list(self.input_names)
        return nc


class Sem:
    __slots__ = ("h", "n")

    def __init__(self, h):
        self.h = h
        self.n = 0

    def inc(self, k=1):
        self.n += k
        return self.n




class PhaseAlloc:

    def __init__(self, p, arena_elems_bf16=94720, name="A"):
        self.p = p
        self.arena = p.sb(name, [128, arena_elems_bf16], BF16)
        self.banks = [p.psum(f"{name}_pb{i}", [128, 512], F32) for i in range(8)]
        self.off = 0
        self.bank_i = 0
        self.cap = arena_elems_bf16

    def reset(self):
        self.off = 0
        self.bank_i = 0

    def sb(self, name, shape, dt=F32):
        rows, cols = shape
        elems = cols * (2 if dt == F32 else 1)
        off = (self.off + 1) & ~1
        assert off + elems <= self.cap, f"arena overflow at {name}: {off}+{elems}>{self.cap}"
        self.off = off + elems
        ap = self.arena.ap[0:rows, off:off + elems]
        if dt == F32:
            ap = self.arena.ap[:, off:off + elems].bitcast(F32)[0:rows, :]
        elif dt != BF16:
            ap = self.arena.ap[:, off:off + elems].bitcast(dt)[0:rows, :]
        return Buf(ap, name)

    def psum(self, name, shape, dt=F32):
        b = self.banks[self.bank_i]
        self.bank_i += 1
        assert self.bank_i <= 8, "out of psum banks"
        return Buf(b.ap, name, st=b.st)


def load_shared_consts(p, NH_E32=32):
    E = p.emit
    names = {
        "iden": ([128, 128], F32), "iden16": ([128, 128], BF16),
        "i32": ([32, 32], F32), "ones32": ([32, 128], F32),
        "onescol16": ([128, 1], BF16), "onesrow16": ([1, 128], BF16),
        "E32": ([32, NH_E32 * 128], F32), "E8": ([8, 8 * 128], F32),
    }
    out = {}
    for n, (shape, dt) in names.items():
        hb = p.dram(n, shape, dt, kind="ExternalInput")
        sbb = p.sb(n + "S", shape, dt)
        E("sync", lambda e, d=sbb, s=hb: e.dma_start(out=d.ap, in_=s.ap),
          reads=[hb], writes=[sbb], dma=True)
        out[n] = sbb
    cstS = p.sb("cstS", [128, 4], F32)
    E("vector", lambda e: e.memset(cstS[:, 0:1], 1e-5), writes=[cstS])
    E("vector", lambda e: e.memset(cstS[:, 1:2], 1.0 / math.sqrt(2.0)), writes=[cstS])
    E("vector", lambda e: e.memset(cstS[:, 2:3], 1e-12), writes=[cstS])
    out["cst"] = cstS
    return out


def phase_barrier(p):
    allsems = list(p.sems.values()) + p.ccsems
    for eng in p.ENGINES:
        p.q[eng].append(lambda e, wv=tuple((s.h, s.n) for s in allsems if s.n > 0):
                        [e.wait_ge(h, n) for (h, n) in wv])
        p.hw[eng] = {s2: s2.n for s2 in allsems}



def host_consts(NH):
    bf = ml_bf16()
    E = np.zeros((32, NH * 128), np.float32)
    for h in range(min(NH, 32)):
        E[h, h * 128:(h + 1) * 128] = 1.0
    E8 = np.zeros((8, 8 * 128), np.float32)
    for h in range(8):
        E8[h, h * 128:(h + 1) * 128] = 1.0
    return {
        "iden": np.eye(128, dtype=np.float32),
        "iden16": np.eye(128, dtype=bf),
        "E8": E8,
        "i32": np.eye(32, dtype=np.float32),
        "ones32": np.ones((32, 128), np.float32),
        "onescol16": np.ones((128, 1), bf),
        "onesrow16": np.ones((1, 128), bf),
        "E32": E,
    }



def emit_mixer0(p, A, sh, io, T=4096, NH=32, pref="m0_", debug=False):
    C = 128
    NCH = T // C
    G = 512
    NG = T // G
    CPG = G // C
    NF = NH * 16
    NFC = NF // 128
    HPG = 4
    NHG = NH // HPG

    sb, psum = A.sb, A.psum
    def dram(name, shape, dt=F32, **kw):
        return p.dram(pref + name, shape, dt, **kw)

    x_h = io["x"]
    w_h = {n: io[n] for n in ("qw", "kw", "vw")}
    iw_h, fw_h = io["iw"], io["fw"]
    b_h = {n: io[n] for n in ("qb", "kb", "vb")}
    ib_h, fb_h = io["ib"], io["fb"]
    gam_h, bet_h = io["gamma"], io["beta"]
    hout_views = io.get("hout_views")
    if hout_views is None:
        hv = io["hout"]
        hout_views = [hv.view(hv.ap[:, 0:T // 2], "hv0"), hv.view(hv.ap[:, T // 2:T], "hv1")]
    w16_h = {n: dram(n + "16", [NF, D], BF16) for n in ("qw", "kw", "vw")}
    fmscr_h = {n: dram(n + "scr", [NFC * 128, 512], BF16) for n in ("q", "k", "v")}
    dbg = {}
    if debug:
        dbg["xn"] = dram("dbg_xn", [128, NDM * T], BF16, kind="ExternalOutput")
        dbg["gB"] = dram("dbg_gB", [32, T], F32, kind="ExternalOutput")
        dbg["gAn"] = dram("dbg_gAn", [32, T], F32, kind="ExternalOutput")
        dbg["aends"] = dram("dbg_aends", [32, NCH + 1], F32, kind="ExternalOutput")
        dbg["qfmH"] = dram("dbg_qfmH", [16, NH * 128], BF16, kind="ExternalOutput")
        dbg["kfmH"] = dram("dbg_kfmH", [16, NH * 128], BF16, kind="ExternalOutput")
        dbg["vfmH"] = dram("dbg_vfmH", [16, NH * 128], BF16, kind="ExternalOutput")
        dbg["k_tm"] = dram("dbg_ktm", [128, CPG * NF], BF16, kind="ExternalOutput")
        dbg["v_tm"] = dram("dbg_vtm", [128, CPG * NF], BF16, kind="ExternalOutput")
        dbg["expw"] = dram("dbg_expw", [128, HPG * 128], F32, kind="ExternalOutput")
        dbg["rho"] = dram("dbg_rho", [128, 32], F32, kind="ExternalOutput")
        dbg["dec"] = dram("dbg_dec", [16, 32], F32, kind="ExternalOutput")
        dbg["st16"] = dram("dbg_st16", [16, NH * 17], BF16, kind="ExternalOutput")
    if16_h = dram("if16", [2 * NH, D], BF16)

    xn_bf = sb("xn_bf", [128, NDM * T], BF16)
    _wstg_holder = []
    qwT = sb("qwT", [128, NDM * NF], BF16)
    kwT = sb("kwT", [128, NDM * NF], BF16)
    vwT = sb("vwT", [128, NDM * NF], BF16)
    ifwT = sb("ifwT", [128, NDM * 2 * NH], BF16)
    bT = {n: sb(n + "T", [128, NFC], F32) for n in ("qb", "kb", "vb")}
    ifbT = sb("ifbT", [32, 2], F32)
    gamT = sb("gamT", [128, NDM], F32)
    betT = sb("betT", [128, NDM], F32)
    brow16 = {n: sb(n + "r16", [1, NF], BF16) for n in ("kb", "vb")}
    iden, iden16 = sh["iden"], sh["iden16"]
    i32, ones32 = sh["i32"], sh["ones32"]
    onescol16, onesrow16 = sh["onescol16"], sh["onesrow16"]
    E32 = sh["E32"]
    cst = sh["cst"]
    _x_tm0 = sb("x_tm0", [128, D], F32)
    x_tm = [_x_tm0, _x_tm0]
    sq = sb("sq", [128, D], F32)
    xnt = sq
    stat = sb("stat", [128, 8], F32)
    g_i = sb("g_i", [32, T], F32)
    g_f = sb("g_f", [32, T], F32)
    a_ends = sb("a_ends", [32, NCH + 1], F32)
    declog = sb("declog", [32, NCH], F32)
    arena = sb("arena", [128, max(3 * NH * C, T * 4)], BF16)
    qfmH = arena.view(arena.ap[0:16, 0 * NH * C:1 * NH * C], "qfmH")
    kfmH = arena.view(arena.ap[0:16, 1 * NH * C:2 * NH * C], "kfmH")
    vfmH = arena.view(arena.ap[0:16, 2 * NH * C:3 * NH * C], "vfmH")
    g_F = arena.view(arena.ap[0:32, 0:T * 2].bitcast(F32), "g_F")
    g_m = arena.view(arena.ap[0:32, T * 2:T * 4].bitcast(F32), "g_m")
    wstg = arena.view(arena.ap[:, 0:2048].bitcast(F32), "wstg")
    wstg16 = arena.view(arena.ap[:, 2048:3072], "wstg16")
    brow = arena.view(arena.ap[:, 3072:5120].bitcast(F32)[0:1, :], "brow")
    stg = sb("stg", [128, G], BF16)
    k_tm = sb("k_tm", [128, CPG * NF], BF16)
    v_tm = sb("v_tm", [128, CPG * NF], BF16)
    expw = sb("expw", [128, HPG * C], F32)
    pn = sb("pn", [128, HPG * C], BF16)
    pd = sb("pd", [128, HPG * C], BF16)
    vscale = sb("vscale", [128, 16], BF16)
    ecol16 = sb("ecol16", [128, 1], BF16)
    state = sb("state", [16, NH * 17], F32)
    state16 = sb("state16", [16, NH * 17], BF16)
    intra = sb("intra", [128, 17], F32)
    tot = sb("tot", [128, 17], F32)
    dsc = sb("dsc", [128, 2], F32)
    rho = sb("rho", [128, 32], F32)
    dec = sb("dec", [16, 32], F32)
    diag = sb("diag", [32, 64], F32)
    _h0 = sb("h_tm0", [128, NF], BF16)
    h_tm = [_h0, _h0]
    h_fm = sb("h_fm", [128, NF], BF16)

    ps_tr = psum("ps_tr", [128, 512], F32)
    ps_st = psum("ps_st", [128, 512], F32)
    ps_g = psum("ps_g", [32, 512], F32)
    ps_p = psum("ps_p", [128, 512], F32)
    ps_d = psum("ps_d", [128, HPG * C], F32)
    ps_sn = psum("ps_sn", [128, HPG * C], F32)
    ps_sd = psum("ps_sd", [128, HPG * C], F32)
    ps_m = psum("ps_m", [128, 512], F32)

    E = p.emit

    def bias_cols(src, width, colbuf, colw, scale=None, also16=None, scale16=None):
        E("sync", lambda e, s=src, w=width: e.dma_start(out=brow[0:1, :w], in_=s[:]),
          reads=[src], writes=[brow], dma=True)
        if also16 is not None:
            if scale16 is None:
                E("vector", lambda e, d=also16, w=width: e.tensor_copy(d[0:1, :w], brow[0:1, :w]),
                  reads=[brow], writes=[also16])
            else:
                E("vector", lambda e, d=also16, w=width, sc=scale16: e.tensor_scalar_mul(
                    d[0:1, :w], brow[0:1, :w], sc), reads=[brow], writes=[also16])
        nch = width // 128
        if nch == 0:
            E("tensor", lambda e, w=width: e.transpose(ps_st[0:w, 0:1], brow[0:1, :w], iden[0:1, 0:1]),
              reads=[brow, iden], writes=[ps_st])
            if scale is None:
                E("vector", lambda e, cb=colbuf, w=width: e.tensor_copy(cb.ap, ps_st[0:w, 0:1]),
                  reads=[ps_st], writes=[colbuf.buf])
            else:
                E("vector", lambda e, cb=colbuf, w=width, sc=scale: e.tensor_scalar_mul(
                    cb.ap, ps_st[0:w, 0:1], sc), reads=[ps_st], writes=[colbuf.buf])
            return
        for j in range(nch):
            E("tensor", lambda e, j=j: e.transpose(ps_st[:, j:j + 1], brow[0:1, j * 128:(j + 1) * 128],
                                                   iden[0:1, 0:1]),
              reads=[brow, iden], writes=[ps_st])
        for j in range(nch):
            if scale is None:
                E("vector", lambda e, cb=colbuf, j=j: e.tensor_copy(cb[:, j:j + 1], ps_st[:, j:j + 1]),
                  reads=[ps_st], writes=[colbuf])
            else:
                E("vector", lambda e, cb=colbuf, j=j, sc=scale: e.tensor_scalar_mul(
                    cb[:, j:j + 1], ps_st[:, j:j + 1], sc), reads=[ps_st], writes=[colbuf])

    class _ColView:
        def __init__(self, buf, ap):
            self.buf = buf
            self.ap = ap

    bias_cols(b_h["qb"], NF, bT["qb"], NFC)
    bias_cols(b_h["kb"], NF, bT["kb"], NFC, scale=0.25,
              also16=brow16["kb"], scale16=0.25)
    bias_cols(b_h["vb"], NF, bT["vb"], NFC, also16=brow16["vb"])
    bias_cols(ib_h, NH, _ColView(ifbT, ifbT[0:NH, 0:1]), 1)
    bias_cols(fb_h, NH, _ColView(ifbT, ifbT[0:NH, 1:2]), 1)
    bias_cols(gam_h, D, gamT, NDM)
    bias_cols(bet_h, D, betT, NDM)

    def repack(src, dst16, rows, dst_off=0, scale=None):
        for r0 in range(0, rows, 128):
            rr = min(128, rows - r0)
            E("sync", lambda e, s=src, r0=r0, rr=rr: e.dma_start(out=wstg[0:rr, :], in_=s[r0:r0 + rr, :]),
              reads=[src], writes=[wstg], dma=True)
            if scale is None:
                E("vector", lambda e, rr=rr: e.tensor_copy(wstg16[0:rr, :], wstg[0:rr, :]),
                  reads=[wstg], writes=[wstg16])
            else:
                E("vector", lambda e, rr=rr, sc=scale: e.tensor_scalar_mul(
                    wstg16[0:rr, :], wstg[0:rr, :], sc), reads=[wstg], writes=[wstg16])
            E("scalar", lambda e, d=dst16, r0=r0 + dst_off, rr=rr: e.dma_start(
                out=d[r0:r0 + rr, :], in_=wstg16[0:rr, :]),
              reads=[wstg16], writes=[dst16], dma=True)

    repack(w_h["qw"], w16_h["qw"], NF)
    repack(w_h["kw"], w16_h["kw"], NF, scale=0.25)
    repack(w_h["vw"], w16_h["vw"], NF)
    repack(iw_h, if16_h, NH)
    repack(fw_h, if16_h, NH, dst_off=NH)

    for j in range(NDM):
        for (wT, src) in ((qwT, w16_h["qw"]), (kwT, w16_h["kw"]), (vwT, w16_h["vw"])):
            E("sync", lambda e, wT=wT, s=src, j=j: e.dma_start_transpose(
                wT[:, j * NF:(j + 1) * NF], s[:, j * 128:(j + 1) * 128]),
              reads=[src], writes=[wT], dma=True)
        E("sync", lambda e, j=j: e.dma_start_transpose(
            ifwT[:, j * 2 * NH:(j + 1) * 2 * NH], if16_h[:, j * 128:(j + 1) * 128]),
          reads=[if16_h], writes=[ifwT], dma=True)

    for i in range(NCH):
        buf = x_tm[i % 2]
        E("sync", lambda e, b=buf, i=i: e.dma_start(out=b[:], in_=x_h[i * 128:(i + 1) * 128, :]),
          reads=[x_h], writes=[buf], dma=True)
        E("vector", lambda e, b=buf: e.reduce_sum(stat[:, 0:1], b[:], axis=AX.X),
          reads=[buf], writes=[stat])
        E("vector", lambda e, b=buf: e.tensor_mul(sq[:], b[:], b[:]), reads=[buf], writes=[sq])
        E("vector", lambda e: e.reduce_sum(stat[:, 1:2], sq[:], axis=AX.X),
          reads=[sq], writes=[stat])
        E("vector", lambda e: e.tensor_scalar_mul(stat[:, 2:3], stat[:, 0:1], 1.0 / D),
          reads=[stat], writes=[stat])
        E("vector", lambda e: e.tensor_mul(stat[:, 3:4], stat[:, 2:3], stat[:, 2:3]),
          reads=[stat], writes=[stat])
        E("vector", lambda e: e.scalar_tensor_tensor(
            stat[:, 4:5], stat[:, 1:2], 1.0 / D, stat[:, 3:4], ALU.mult, ALU.subtract),
          reads=[stat], writes=[stat])
        E("scalar", lambda e: e.activation(stat[:, 5:6], stat[:, 4:5], AF.Ln, bias=cst[:, 0:1], scale=1.0),
          reads=[stat, cst], writes=[stat])
        E("scalar", lambda e: e.activation(stat[:, 6:7], stat[:, 5:6], AF.Exp, bias=0.0, scale=-0.5),
          reads=[stat], writes=[stat])
        E("vector", lambda e, b=buf: e.tensor_scalar(
            xnt[:], b[:], stat[:, 2:3], stat[:, 6:7], ALU.subtract, ALU.mult),
          reads=[buf, stat], writes=[xnt])
        for j in range(NDM):
            bs = j % 4
            E("tensor", lambda e, j=j, bs=bs: e.transpose(
                ps_tr[:, bs * 128:(bs + 1) * 128], xnt[:, j * 128:(j + 1) * 128], iden[:]),
              reads=[xnt, iden], writes=[ps_tr])
            E("vector", lambda e, j=j, i=i, bs=bs: e.tensor_scalar(
                xn_bf[:, j * T + i * 128: j * T + (i + 1) * 128],
                ps_tr[:, bs * 128:(bs + 1) * 128],
                gamT[:, j:j + 1], betT[:, j:j + 1], ALU.mult, ALU.add),
              reads=[ps_tr, gamT, betT], writes=[xn_bf])

    for g in range(NG):
        for w in range(2):
            for j in range(NDM):
                E("tensor", lambda e, j=j, g=g, w=w: e.matmul(
                    ps_g[0:NH, :G],
                    ifwT[:, j * 2 * NH + w * NH: j * 2 * NH + (w + 1) * NH],
                    xn_bf[:, j * T + g * G: j * T + (g + 1) * G],
                    start=(j == 0), stop=(j == NDM - 1)),
                  reads=[ifwT, xn_bf], writes=[ps_g])
            dst = g_i if w == 0 else g_f
            E("vector", lambda e, d=dst, g=g, w=w: e.tensor_scalar(
                d[0:NH, g * G:(g + 1) * G], ps_g[0:NH, :G], ifbT[0:NH, w:w + 1], None, ALU.add),
              reads=[ps_g, ifbT], writes=[dst])

    E("vector", lambda e: e.memset(stat[:, 7:8], 0.0), writes=[stat])
    E("vector", lambda e: e.tensor_tensor_scan(
        g_F[0:NH, :], g_f[0:NH, :], stat[0:NH, 7:8].broadcast_to([NH, T]), 0.0, ALU.add, ALU.add),
      reads=[g_f, stat], writes=[g_F])
    E("vector", lambda e: e.tensor_tensor_scan(
        g_m[0:NH, :], g_f[0:NH, :], g_i[0:NH, :], 0.0, ALU.add, ALU.max),
      reads=[g_f, g_i], writes=[g_m])
    E("vector", lambda e: e.tensor_sub(g_i[0:NH, :], g_i[0:NH, :], g_F[0:NH, :]),
      reads=[g_i, g_F], writes=[g_i])
    E("vector", lambda e: e.tensor_sub(g_f[0:NH, :], g_F[0:NH, :], g_m[0:NH, :]),
      reads=[g_F, g_m], writes=[g_f])
    gB, gAn = g_i, g_f
    E("vector", lambda e: e.memset(a_ends[0:NH, 0:1], 0.0), writes=[a_ends])
    E("vector", lambda e: e.tensor_scalar_mul(
        a_ends[0:NH, 1:NCH + 1],
        gAn.ap[0:NH, :].rearrange("p (c j) -> p c j", j=128)[:, :, 127:128].squeeze(2),
        -1.0),
      reads=[gAn], writes=[a_ends])
    E("vector", lambda e: e.tensor_sub(declog[0:NH, :], a_ends[0:NH, 0:NCH], a_ends[0:NH, 1:NCH + 1]),
      reads=[a_ends], writes=[declog])
    if debug:
        E("scalar", lambda e: e.dma_start(out=dbg["xn"].ap, in_=xn_bf.ap), reads=[xn_bf], writes=[dbg["xn"]], dma=True)
        E("scalar", lambda e: e.dma_start(out=dbg["gB"].ap[0:NH, :], in_=gB[0:NH, :]), reads=[gB], writes=[dbg["gB"]], dma=True)
        E("scalar", lambda e: e.dma_start(out=dbg["gAn"].ap[0:NH, :], in_=gAn[0:NH, :]), reads=[gAn], writes=[dbg["gAn"]], dma=True)
        E("scalar", lambda e: e.dma_start(out=dbg["aends"].ap[0:NH, :], in_=a_ends[0:NH, :]), reads=[a_ends], writes=[dbg["aends"]], dma=True)
        E("scalar", lambda e: e.dma_start(out=dbg["declog"].ap[0:NH, :], in_=declog[0:NH, :]), reads=[declog], writes=[dbg["declog"]], dma=True) if "declog" in dbg else None
    E("vector", lambda e: e.memset(state[:], 0.0), writes=[state])
    E("vector", lambda e: e.memset(state16[:], 0.0), writes=[state16])

    for g in range(NG):
        for (nm, wT, bcol, fmH) in (("q", qwT, bT["qb"], qfmH), ("k", kwT, bT["kb"], kfmH),
                                    ("v", vwT, bT["vb"], vfmH)):
            scr = fmscr_h[nm]
            for fc in range(NFC):
                for j in range(NDM):
                    E("tensor", lambda e, wT=wT, fc=fc, g=g, j=j: e.matmul(
                        ps_p[:, :G],
                        wT[:, j * NF + fc * 128: j * NF + (fc + 1) * 128],
                        xn_bf[:, j * T + g * G: j * T + (g + 1) * G],
                        start=(j == 0), stop=(j == NDM - 1)),
                      reads=[wT, xn_bf], writes=[ps_p])
                E("vector", lambda e, bcol=bcol, fc=fc: e.tensor_scalar(
                    stg[:, :G], ps_p[:, :G], bcol[:, fc:fc + 1], None, ALU.add),
                  reads=[ps_p, bcol], writes=[stg])
                E("scalar", lambda e, scr=scr, fc=fc: e.dma_start(
                    out=scr.ap[fc * 128:(fc + 1) * 128, :G], in_=stg[:, :G]),
                  reads=[stg], writes=[scr], dma=True)
        for cc in range(CPG):
            ch = g * CPG + cc
            for (wT, dst_tm, br16) in ((kwT, k_tm, brow16["kb"]), (vwT, v_tm, brow16["vb"])):
                for j in range(NDM):
                    E("tensor", lambda e, wT=wT, ch=ch, j=j: e.matmul(
                        ps_p[:, :NF],
                        xn_bf[:, j * T + ch * C: j * T + (ch + 1) * C],
                        wT[:, j * NF:(j + 1) * NF],
                        start=(j == 0), stop=False),
                      reads=[wT, xn_bf], writes=[ps_p])
                E("tensor", lambda e, br=br16: e.matmul(
                    ps_p[:, :NF], onesrow16[0:1, :], br[0:1, :NF], start=False, stop=True),
                  reads=[onesrow16, br16], writes=[ps_p])
                E("vector", lambda e, d=dst_tm, cc=cc: e.tensor_copy(
                    d[:, cc * NF:(cc + 1) * NF], ps_p[:, :NF]),
                  reads=[ps_p], writes=[dst_tm])

        if debug and g == 0:
            for nmb, srcb in (("k_tm", k_tm), ("v_tm", v_tm)):
                E("scalar", lambda e, d=dbg[nmb], s=srcb: e.dma_start(out=d.ap, in_=s.ap),
                  reads=[srcb], writes=[dbg[nmb]], dma=True)
        for cc in range(CPG):
            ch = g * CPG + cc
            for (nm, fmH) in (("q", qfmH), ("k", kfmH), ("v", vfmH)):
                scr = fmscr_h[nm]
                for fc in range(NFC):
                    E("sync", lambda e, scr=scr, fmH=fmH, fc=fc, cc=cc: e.dma_start(
                        out=fmH.ap[0:16, :].rearrange("d (h t) -> d h t", t=C)
                            [:, fc * 8:(fc + 1) * 8, :],
                        in_=scr.ap[fc * 128:(fc + 1) * 128, cc * C:(cc + 1) * C]
                            .rearrange("(h d) t -> d h t", d=16)),
                      reads=[scr], writes=[fmH], dma=True)
            if debug and ch == 0:
                for nmb, srcb in (("qfmH", qfmH), ("kfmH", kfmH), ("vfmH", vfmH)):
                    E("scalar", lambda e, d=dbg[nmb], s=srcb: e.dma_start(out=d.ap[0:16, :], in_=s[0:16, :]),
                      reads=[srcb], writes=[dbg[nmb]], dma=True)
            E("vector", lambda e, ch=ch: e.tensor_scalar(
                diag[0:32, 0:32], i32[:], a_ends[0:NH, ch:ch + 1], None, ALU.mult),
              reads=[i32, a_ends], writes=[diag])
            E("tensor", lambda e, ch=ch: e.transpose(
                ps_m[:, 96:128], gAn[0:NH, ch * C:(ch + 1) * C], i32[0:NH, 0:NH]),
              reads=[gAn, i32], writes=[ps_m])
            E("tensor", lambda e: e.matmul(
                ps_m[:, 96:128], ones32[:, 0:128], diag[0:32, 0:32],
                start=False, stop=True, skip_group_check=True),
              reads=[ones32, diag], writes=[ps_m])
            E("scalar", lambda e: e.activation(rho[:, 0:32], ps_m[:, 96:128], AF.Exp,
                                               bias=0.0, scale=1.0),
              reads=[ps_m], writes=[rho])
            E("vector", lambda e, ch=ch: e.tensor_scalar(
                diag[0:32, 32:64], i32[:], declog[0:NH, ch:ch + 1], None, ALU.mult),
              reads=[i32, declog], writes=[diag])
            E("tensor", lambda e: e.matmul(
                ps_m[0:16, 128:160], ones32[:, 0:16], diag[0:32, 32:64], start=True, stop=True),
              reads=[ones32, diag], writes=[ps_m])
            E("scalar", lambda e: e.activation(dec[0:16, 0:32], ps_m[0:16, 128:160], AF.Exp,
                                               bias=0.0, scale=1.0),
              reads=[ps_m], writes=[dec])

            hbuf = h_tm[ch % 2]
            for hg in range(NHG):
                h0 = hg * HPG
                E("tensor", lambda e, ch=ch, h0=h0: e.matmul(
                    ps_d[:, :], gB[0:NH, ch * C:(ch + 1) * C],
                    E32[0:NH, h0 * 128:(h0 + HPG) * 128], start=True, stop=False),
                  reads=[gB, E32], writes=[ps_d])
                for bi in range(HPG):
                    E("tensor", lambda e, ch=ch, h=h0 + bi, bi=bi: e.matmul(
                        ps_d[:, bi * C:(bi + 1) * C],
                        E32[0:NH, h * 128:(h + 1) * 128],
                        gAn[0:NH, ch * C:(ch + 1) * C],
                        start=False, stop=(bi == HPG - 1)),
                      reads=[E32, gAn], writes=[ps_d])
                for bi in range(HPG):
                    h = h0 + bi
                    E("tensor", lambda e, cc=cc, h=h, bi=bi: e.matmul(
                        ps_sn[:, bi * C:(bi + 1) * C],
                        vfmH[0:16, h * C:(h + 1) * C],
                        qfmH[0:16, h * C:(h + 1) * C],
                        start=True, stop=True),
                      reads=[vfmH, qfmH], writes=[ps_sn])
                    E("tensor", lambda e, cc=cc, h=h, bi=bi: e.matmul(
                        ps_sd[:, bi * C:(bi + 1) * C],
                        kfmH[0:16, h * C:(h + 1) * C],
                        qfmH[0:16, h * C:(h + 1) * C],
                        start=True, stop=True),
                      reads=[kfmH, qfmH], writes=[ps_sd])
                E("vector", lambda e: e.tensor_scalar_min(expw[:], ps_d[:], 0.0),
                  reads=[ps_d], writes=[expw])
                E("scalar", lambda e: e.activation(expw[:], expw[:], AF.Exp, bias=0.0, scale=1.0),
                  reads=[expw], writes=[expw])
                E("gpsimd", lambda e: e.affine_select(
                    expw[:], expw[:], [[0, HPG], [1, C]], ALU.is_ge, 0.0,
                    base=0, channel_multiplier=-1),
                  reads=[expw], writes=[expw])
                if debug and ch == 0 and hg == 0:
                    E("scalar", lambda e: e.dma_start(out=dbg["expw"].ap, in_=expw.ap),
                      reads=[expw], writes=[dbg["expw"]], dma=True)
                    E("scalar", lambda e: e.dma_start(out=dbg["rho"].ap, in_=rho.ap),
                      reads=[rho], writes=[dbg["rho"]], dma=True)
                    E("scalar", lambda e: e.dma_start(out=dbg["dec"].ap[0:16, :], in_=dec[0:16, :]),
                      reads=[dec], writes=[dbg["dec"]], dma=True)
                E("vector", lambda e: e.tensor_mul(pn[:], expw[:], ps_sn[:]),
                  reads=[expw, ps_sn], writes=[pn])
                E("vector", lambda e: e.tensor_mul(pd[:], expw[:], ps_sd[:]),
                  reads=[expw, ps_sd], writes=[pd])
                for bi in range(HPG):
                    h = h0 + bi
                    E("tensor", lambda e, cc=cc, h=h: e.matmul(
                        ps_m[:, 32:49],
                        qfmH[0:16, h * C:(h + 1) * C],
                        state16[0:16, h * 17:(h + 1) * 17], start=True, stop=True),
                      reads=[qfmH, state16], writes=[ps_m])
                    E("tensor", lambda e, cc=cc, h=h, bi=bi: e.matmul(
                        ps_m[:, 0:16], pn[:, bi * C:(bi + 1) * C],
                        k_tm[:, cc * NF + h * 16: cc * NF + (h + 1) * 16],
                        start=True, stop=True),
                      reads=[pn, k_tm], writes=[ps_m])
                    E("tensor", lambda e, bi=bi: e.matmul(
                        ps_m[:, 16:17], pd[:, bi * C:(bi + 1) * C], onescol16[:],
                        start=True, stop=True),
                      reads=[pd, onescol16], writes=[ps_m])
                    E("scalar", lambda e: e.activation(intra[:], ps_m[:, 0:17], AF.Copy,
                                                       bias=0.0, scale=1.0),
                      reads=[ps_m], writes=[intra])
                    E("vector", lambda e, h=h: e.scalar_tensor_tensor(
                        tot[:], ps_m[:, 32:49], rho[:, h:h + 1], intra[:], ALU.mult, ALU.add),
                      reads=[ps_m, rho, intra], writes=[tot])
                    E("vector", lambda e, cc=cc, h=h, bi=bi: e.tensor_scalar(
                        vscale[:], v_tm[:, cc * NF + h * 16: cc * NF + (h + 1) * 16],
                        expw[:, bi * C + 127: bi * C + 128], None, ALU.mult),
                      reads=[v_tm, expw], writes=[vscale])
                    E("vector", lambda e, bi=bi: e.tensor_copy(
                        ecol16[:], expw[:, bi * C + 127: bi * C + 128]),
                      reads=[expw], writes=[ecol16])
                    E("tensor", lambda e, cc=cc, h=h: e.matmul(
                        ps_m[0:16, 64:80], vscale[:],
                        k_tm[:, cc * NF + h * 16: cc * NF + (h + 1) * 16],
                        start=True, stop=True),
                      reads=[vscale, k_tm], writes=[ps_m])
                    E("tensor", lambda e, cc=cc, h=h: e.matmul(
                        ps_m[0:16, 80:81], k_tm[:, cc * NF + h * 16: cc * NF + (h + 1) * 16],
                        ecol16[:], start=True, stop=True),
                      reads=[k_tm, ecol16], writes=[ps_m])
                    E("vector", lambda e, h=h: e.scalar_tensor_tensor(
                        state[0:16, h * 17:(h + 1) * 17], state[0:16, h * 17:(h + 1) * 17],
                        dec[0:16, h:h + 1], ps_m[0:16, 64:81], ALU.mult, ALU.add),
                      reads=[state, dec, ps_m], writes=[state]),
                    E("vector", lambda e, h=h: e.tensor_copy(
                        state16[0:16, h * 17:(h + 1) * 17], state[0:16, h * 17:(h + 1) * 17]),
                      reads=[state], writes=[state16])
                    E("scalar", lambda e: e.activation(dsc[:, 0:1], tot[:, 16:17], AF.Abs,
                                                       bias=0.0, scale=1.0),
                      reads=[tot], writes=[dsc])
                    E("vector", lambda e: e.tensor_scalar_max(dsc[:, 0:1], dsc[:, 0:1], 1.0),
                      reads=[dsc], writes=[dsc])
                    E("vector", lambda e: e.reciprocal(dsc[:, 1:2], dsc[:, 0:1]),
                      reads=[dsc], writes=[dsc])
                    E("vector", lambda e, h=h, hb=hbuf: e.tensor_scalar(
                        hb[:, h * 16:(h + 1) * 16], tot[:, 0:16], dsc[:, 1:2], None, ALU.mult),
                      reads=[tot, dsc], writes=[hbuf])
            for fc2 in range(NFC):
                E("tensor", lambda e, hb=hbuf, fc2=fc2: e.transpose(
                    ps_tr.ap.bitcast(BF16)[:, fc2 * 128:(fc2 + 1) * 128],
                    hb[:, fc2 * 128:(fc2 + 1) * 128], iden16[:]),
                  reads=[hbuf, iden16], writes=[ps_tr])
            for fc2 in range(NFC):
                E("vector", lambda e, fc2=fc2: e.tensor_copy(
                    h_fm[:, fc2 * 128:(fc2 + 1) * 128],
                    ps_tr.ap.bitcast(BF16)[:, fc2 * 128:(fc2 + 1) * 128]),
                  reads=[ps_tr], writes=[h_fm])
            hv = hout_views[ch // (NCH // 2)]
            hoff = (ch * C) % (T // 2)
            E("scalar", lambda e, hv=hv, hoff=hoff: e.dma_start(
                out=hv.ap.rearrange("(fc p) t -> p fc t", p=128)[:, :, hoff:hoff + C],
                in_=h_fm.ap.rearrange("p (fc t) -> p fc t", t=C)),
              reads=[h_fm], writes=[hv], dma=True)

    if debug:
        E("scalar", lambda e: e.dma_start(out=dbg["st16"].ap[0:16, :], in_=state16[0:16, :]),
          reads=[state16], writes=[dbg["st16"]], dma=True)
    return


def build_mixer0(T=4096, NH=32, debug=False):
    nc = bass.Bass("TRN2", target_bir_lowering=False)
    p = Prog(nc)
    sh = load_shared_consts(p, NH_E32=NH)
    A = PhaseAlloc(p)
    NF = NH * 16
    io = {}
    io["x"] = p.dram("x", [T, D], F32, kind="ExternalInput")
    for n in ("qw", "kw", "vw"):
        io[n] = p.dram(n, [NF, D], F32, kind="ExternalInput")
    io["iw"] = p.dram("iw", [NH, D], F32, kind="ExternalInput")
    io["fw"] = p.dram("fw", [NH, D], F32, kind="ExternalInput")
    for n in ("qb", "kb", "vb"):
        io[n] = p.dram(n, [1, NF], F32, kind="ExternalInput")
    io["ib"] = p.dram("ib", [1, NH], F32, kind="ExternalInput")
    io["fb"] = p.dram("fb", [1, NH], F32, kind="ExternalInput")
    io["gamma"] = p.dram("gamma", [1, D], F32, kind="ExternalInput")
    io["beta"] = p.dram("beta", [1, D], F32, kind="ExternalInput")
    io["hout"] = p.dram("hout", [NF, T], BF16, kind="ExternalOutput")
    emit_mixer0(p, A, sh, io, T=T, NH=NH, debug=debug)
    return p.finish()





def emit_postffn0(p, A, sh, io, TK=2048, NE=8, pref="p0_", between=None):
    NT = TK // 128
    NTB = TK // 512
    HID = 2816
    NHC = HID // 128

    sb, psum = A.sb, A.psum
    def dram(name, shape, dt=F32, **kw):
        return p.dram(pref + name, shape, dt, **kw)

    x_h = io["x"]
    h0_h = io["h0fm"]
    ow_h, ob_h, gw_h = io["ow"], io["ob"], io["gate_w"]
    w1_h, w3_h, w2_h = io["w1"], io["w3"], io["w2"]
    x2_h = io["x2"]

    ow16_h = dram("ow16", [D, D], BF16)
    gw16_h = dram("gw16", [8, D], BF16)
    w1n_h = dram("w1n", [2, HID, D], BF16)
    w3n_h = dram("w3n", [2, HID, D], BF16)
    w2n_h = dram("w2n", [2, D, HID], BF16)
    w1t_h = dram("w1t", [2, D, HID], BF16)
    w3t_h = dram("w3t", [2, D, HID], BF16)
    w2t_h = dram("w2t", [2, HID, D], BF16)
    x1d_h = dram("x1d", [TK, D], F32)
    acc_h = io.get("acc_out") or dram("accd", [D, TK], F32)
    acc_rd = io.get("acc_in") or acc_h

    owT = sb("owT", [128, NDM * D], BF16)
    gwT = sb("gwT", [128, NDM * 8], BF16)
    obr16 = sb("obr16", [1, D], BF16)
    brow = sb("brow", [1, D], F32)
    iden, iden16 = sh["iden"], sh["iden16"]
    onesrow16, E8 = sh["onesrow16"], sh["E8"]
    cst = sh["cst"]
    wstg = sb("wstg", [128, 1408], F32)
    wstg16 = sb("wstg16", [128, 1408], BF16)
    tstg = sb("tstg", [128, HID], BF16)
    w1hc = [sb(f"w1hc{i}", [128, NDM * 128], BF16) for i in range(2)]
    w3hc = [sb(f"w3hc{i}", [128, NDM * 128], BF16) for i in range(2)]
    w2hc = [sb(f"w2hc{i}", [128, D], BF16) for i in range(2)]
    x_t = sb("x_t", [128, D], F32)
    _hf0 = sb("hf_t0", [128, NDM * 128], BF16)
    hf_t = [_hf0, _hf0]
    x1t = sb("x1t", [128, D], F32)
    x1c = sb("x1c", [128, D], BF16)
    x1fm = sb("x1fm", [128, NDM * TK], BF16)
    stat = sb("stat", [128, 8], F32)
    lg = sb("lg", [128, 8], F32)
    cwk = sb("cwk", [128, 24], F32)
    c_tm = sb("c_tm", [128, 8], F32)
    c_em = sb("c_em", [8, TK], F32)
    c_bc = sb("c_bc", [128, 512], F32)
    sg = sb("sg", [128, 512], F32)
    sg2 = sb("sg2", [128, 512], F32)
    hg = sb("hg", [128, NHC * 512], BF16)
    ysc = sb("ysc", [128, 512], F32)
    acc_t = sb("acc_t", [128, D], F32)
    x2t = sb("x2t", [128, D], F32)

    ps_tr = psum("ps_tr", [128, 512], F32)
    ps_a = psum("ps_a", [128, 512], F32)
    ps_b = psum("ps_b", [128, 512], F32)
    ps_h1 = psum("ps_h1", [128, 512], F32)
    ps_h2 = psum("ps_h2", [128, 512], F32)
    ps_y = psum("ps_y", [128, 512], F32)
    ps_c = psum("ps_c", [128, 512], F32)

    E = p.emit

    E("sync", lambda e: e.dma_start(out=brow.ap, in_=ob_h.ap), reads=[ob_h], writes=[brow], dma=True)
    E("vector", lambda e: e.tensor_copy(obr16.ap, brow.ap), reads=[brow], writes=[obr16])

    def cast_tiles(srcbuf, sap, dstbuf, dap, rows, cols):
        for r0 in range(0, rows, 128):
            rr = min(128, rows - r0)
            for c0 in range(0, cols, 1408):
                ccw = min(1408, cols - c0)
                E("scalar", lambda e, s=sap, r0=r0, rr=rr, c0=c0, ccw=ccw: e.dma_start(
                    out=wstg[0:rr, 0:ccw], in_=s[r0:r0 + rr, c0:c0 + ccw]),
                  reads=[srcbuf], writes=[wstg], dma=True)
                E("vector", lambda e, rr=rr, ccw=ccw: e.tensor_copy(
                    wstg16[0:rr, 0:ccw], wstg[0:rr, 0:ccw]), reads=[wstg], writes=[wstg16])
                E("scalar", lambda e, d=dap, r0=r0, rr=rr, c0=c0, ccw=ccw: e.dma_start(
                    out=d[r0:r0 + rr, c0:c0 + ccw], in_=wstg16[0:rr, 0:ccw]),
                  reads=[wstg16], writes=[dstbuf], dma=True)

    cast_tiles(ow_h, ow_h.ap, ow16_h, ow16_h.ap, D, D)
    E("sync", lambda e: e.dma_start(out=wstg[0:8, 0:D], in_=gw_h.ap),
      reads=[gw_h], writes=[wstg], dma=True)
    E("vector", lambda e: e.tensor_copy(wstg16[0:8, 0:D], wstg[0:8, 0:D]),
      reads=[wstg], writes=[wstg16])
    for j in range(NDM):
        E("tensor", lambda e, j=j: e.transpose(
            ps_tr.ap.bitcast(BF16)[:, 0:8], wstg16[0:8, j * 128:(j + 1) * 128], iden16[0:8, 0:8]),
          reads=[wstg16, iden16], writes=[ps_tr])
        E("vector", lambda e, j=j: e.tensor_copy(
            gwT[:, j * 8:(j + 1) * 8], ps_tr.ap.bitcast(BF16)[:, 0:8]),
          reads=[ps_tr], writes=[gwT])
    for j in range(NDM):
        E("sync", lambda e, j=j: e.dma_start_transpose(
            owT[:, j * D:(j + 1) * D], ow16_h.ap[:, j * 128:(j + 1) * 128]),
          reads=[ow16_h], writes=[owT], dma=True)


    def repack_expert(e):
        s = e % 2
        cast_tiles(w1_h, w1_h.ap[e], w1n_h, w1n_h.ap[s], HID, D)
        cast_tiles(w3_h, w3_h.ap[e], w3n_h, w3n_h.ap[s], HID, D)
        cast_tiles(w2_h, w2_h.ap[e], w2n_h, w2n_h.ap[s], D, HID)
        for j in range(NDM):
            E("scalar", lambda e2, j=j, s=s: e2.dma_start_transpose(
                tstg[:, 0:HID], w1n_h.ap[s][:, j * 128:(j + 1) * 128]),
              reads=[w1n_h], writes=[tstg], dma=True)
            E("scalar", lambda e2, j=j, s=s: e2.dma_start(
                out=w1t_h.ap[s][j * 128:(j + 1) * 128, :], in_=tstg[:, 0:HID]),
              reads=[tstg], writes=[w1t_h], dma=True)
        for j in range(NDM):
            E("scalar", lambda e2, j=j, s=s: e2.dma_start_transpose(
                tstg[:, 0:HID], w3n_h.ap[s][:, j * 128:(j + 1) * 128]),
              reads=[w3n_h], writes=[tstg], dma=True)
            E("scalar", lambda e2, j=j, s=s: e2.dma_start(
                out=w3t_h.ap[s][j * 128:(j + 1) * 128, :], in_=tstg[:, 0:HID]),
              reads=[tstg], writes=[w3t_h], dma=True)
        for j2 in range(NHC):
            E("scalar", lambda e2, j2=j2, s=s: e2.dma_start_transpose(
                tstg[:, 0:D], w2n_h.ap[s][:, j2 * 128:(j2 + 1) * 128]),
              reads=[w2n_h], writes=[tstg], dma=True)
            E("scalar", lambda e2, j2=j2, s=s: e2.dma_start(
                out=w2t_h.ap[s][j2 * 128:(j2 + 1) * 128, :], in_=tstg[:, 0:D]),
              reads=[tstg], writes=[w2t_h], dma=True)

    def ln_inplace(buf, n=D):
        E("vector", lambda e: e.reduce_sum(stat[:, 0:1], buf[:, 0:n], axis=AX.X),
          reads=[buf], writes=[stat])
        E("vector", lambda e: e.tensor_mul(sg[:, 0:1], stat[:, 0:1], stat[:, 0:1]),
          reads=[stat], writes=[sg]) if False else None
        E("vector", lambda e: e.tensor_mul(x1c.ap.bitcast(F32)[:, 0:n // 2], buf[:, 0:n // 2], buf[:, 0:n // 2]),
          reads=[buf], writes=[x1c]) if False else None
        E("scalar", lambda e: e.activation(sg[:, 0:n] if n <= 512 else cwk[:, 0:1], buf[:, 0:n],
                                           AF.Square, bias=0.0, scale=1.0,
                                           accum_out=stat[:, 1:2]) if False else
          e.activation(cwk[:, 8:9], buf[:, 0:1], AF.Square), reads=[buf], writes=[cwk]) if False else None
        E("vector", lambda e: e.tensor_mul(hgsq[:, 0:n], buf[:, 0:n], buf[:, 0:n]),
          reads=[buf], writes=[hgsq])
        E("vector", lambda e: e.reduce_sum(stat[:, 1:2], hgsq[:, 0:n], axis=AX.X),
          reads=[hgsq], writes=[stat])
        E("vector", lambda e: e.tensor_scalar_mul(stat[:, 2:3], stat[:, 0:1], 1.0 / n),
          reads=[stat], writes=[stat])
        E("vector", lambda e: e.tensor_mul(stat[:, 3:4], stat[:, 2:3], stat[:, 2:3]),
          reads=[stat], writes=[stat])
        E("vector", lambda e: e.scalar_tensor_tensor(
            stat[:, 4:5], stat[:, 1:2], 1.0 / n, stat[:, 3:4], ALU.mult, ALU.subtract),
          reads=[stat], writes=[stat])
        E("scalar", lambda e: e.activation(stat[:, 5:6], stat[:, 4:5], AF.Ln, bias=cst[:, 0:1], scale=1.0),
          reads=[stat, cst], writes=[stat])
        E("scalar", lambda e: e.activation(stat[:, 6:7], stat[:, 5:6], AF.Exp, bias=0.0, scale=-0.5),
          reads=[stat], writes=[stat])
        E("vector", lambda e: e.tensor_scalar(
            buf[:, 0:n], buf[:, 0:n], stat[:, 2:3], stat[:, 6:7], ALU.subtract, ALU.mult),
          reads=[buf, stat], writes=[buf])

    hgsq = sb("hgsq", [128, D], F32)

    for ti in range(NT):
        hbuf = hf_t[ti % 2]
        E("sync", lambda e, hb=hbuf, ti=ti: e.dma_start(
            out=hb.ap, in_=h0_h.ap.rearrange("(c p) t -> p c t", p=128)[:, :, ti * 128:(ti + 1) * 128]),
          reads=[h0_h], writes=[hbuf], dma=True)
        E("sync", lambda e, ti=ti: e.dma_start(out=x_t.ap, in_=x_h.ap[ti * 128:(ti + 1) * 128, :]),
          reads=[x_h], writes=[x_t], dma=True)
        for dmb in range(2):
            for hf in range(NDM):
                E("tensor", lambda e, hb=hbuf, hf=hf, dmb=dmb: e.matmul(
                    ps_a.ap[:, 0:512], hb[:, hf * 128:(hf + 1) * 128],
                    owT[:, hf * D + dmb * 512: hf * D + dmb * 512 + 512],
                    start=(hf == 0), stop=False),
                  reads=[hbuf, owT], writes=[ps_a])
            E("tensor", lambda e, dmb=dmb: e.matmul(
                ps_a.ap[:, 0:512], onesrow16[0:1, :], obr16[0:1, dmb * 512:dmb * 512 + 512],
                start=False, stop=True),
              reads=[onesrow16, obr16], writes=[ps_a])
            E("vector", lambda e, dmb=dmb: e.scalar_tensor_tensor(
                x1t[:, dmb * 512:(dmb + 1) * 512], x_t[:, dmb * 512:(dmb + 1) * 512],
                2.0, ps_a.ap[:, 0:512], ALU.mult, ALU.add),
              reads=[x_t, ps_a], writes=[x1t])
        ln_inplace(x1t)
        E("scalar", lambda e, ti=ti: e.dma_start(out=x1d_h.ap[ti * 128:(ti + 1) * 128, :], in_=x1t.ap),
          reads=[x1t], writes=[x1d_h], dma=True)
        E("vector", lambda e: e.tensor_copy(x1c.ap, x1t.ap), reads=[x1t], writes=[x1c])
        for j in range(NDM):
            E("tensor", lambda e, j=j: e.transpose(
                ps_tr.ap.bitcast(BF16)[:, j % 4 * 128:(j % 4 + 1) * 128],
                x1c[:, j * 128:(j + 1) * 128], iden16[:]),
              reads=[x1c, iden16], writes=[ps_tr])
            E("vector", lambda e, j=j, ti=ti: e.tensor_copy(
                x1fm[:, j * TK + ti * 128: j * TK + (ti + 1) * 128],
                ps_tr.ap.bitcast(BF16)[:, j % 4 * 128:(j % 4 + 1) * 128]),
              reads=[ps_tr], writes=[x1fm])
        for j in range(NDM):
            E("tensor", lambda e, j=j, ti=ti: e.matmul(
                ps_b.ap[:, 0:8], x1fm[:, j * TK + ti * 128: j * TK + (ti + 1) * 128],
                gwT[:, j * 8:(j + 1) * 8], start=(j == 0), stop=(j == NDM - 1)),
              reads=[x1fm, gwT], writes=[ps_b])
        E("vector", lambda e: e.tensor_copy(lg.ap, ps_b.ap[:, 0:8]), reads=[ps_b], writes=[lg])
        E("vector", lambda e: e.reduce_max(cwk[:, 0:1], lg.ap, axis=AX.X), reads=[lg], writes=[cwk])
        E("vector", lambda e: e.tensor_scalar(cwk[:, 8:16], lg.ap, cwk[:, 0:1], None, ALU.subtract),
          reads=[lg, cwk], writes=[cwk])
        E("vector", lambda e: e.tensor_scalar(cwk[:, 16:24], cwk[:, 8:16], 0.0, None, ALU.is_equal),
          reads=[cwk], writes=[cwk])
        E("vector", lambda e: e.scalar_tensor_tensor(
            c_tm.ap, cwk[:, 16:24], -1e30, lg.ap, ALU.mult, ALU.add),
          reads=[cwk, lg], writes=[c_tm])
        E("vector", lambda e: e.reduce_max(cwk[:, 1:2], c_tm.ap, axis=AX.X), reads=[c_tm], writes=[cwk])
        E("vector", lambda e: e.tensor_scalar(cwk[:, 2:3], cwk[:, 1:2], cwk[:, 0:1], None, ALU.subtract),
          reads=[cwk], writes=[cwk])
        E("scalar", lambda e: e.activation(cwk[:, 3:4], cwk[:, 2:3], AF.Exp, bias=0.0, scale=1.0),
          reads=[cwk], writes=[cwk])
        E("vector", lambda e: e.tensor_scalar(cwk[:, 4:5], cwk[:, 3:4], 1.0, None, ALU.add),
          reads=[cwk], writes=[cwk])
        E("vector", lambda e: e.reciprocal(cwk[:, 5:6], cwk[:, 4:5]), reads=[cwk], writes=[cwk])
        E("vector", lambda e: e.tensor_mul(cwk[:, 6:7], cwk[:, 3:4], cwk[:, 5:6]),
          reads=[cwk], writes=[cwk])
        E("vector", lambda e: e.tensor_scalar(cwk[:, 8:16], c_tm.ap, cwk[:, 1:2], None, ALU.is_equal),
          reads=[c_tm, cwk], writes=[cwk])
        E("vector", lambda e: e.tensor_scalar(c_tm.ap, cwk[:, 8:16], cwk[:, 6:7], None, ALU.mult),
          reads=[cwk], writes=[c_tm])
        E("vector", lambda e: e.scalar_tensor_tensor(
            c_tm.ap, cwk[:, 16:24], cwk[:, 5:6], c_tm.ap, ALU.mult, ALU.add),
          reads=[cwk, c_tm], writes=[c_tm])
        E("tensor", lambda e: e.transpose(ps_c.ap[0:8, 0:128], c_tm.ap, iden[:]),
          reads=[c_tm, iden], writes=[ps_c])
        E("vector", lambda e, ti=ti: e.tensor_copy(c_em[0:8, ti * 128:(ti + 1) * 128], ps_c.ap[0:8, 0:128]),
          reads=[ps_c], writes=[c_em])

    repack_expert(0)
    for e in range(NE):
        s = e % 2
        if e < NE - 1:
            repack_expert(e + 1)
        for tb in range(NTB):
            E("tensor", lambda e2, e=e, tb=tb: e2.matmul(
                ps_c.ap[:, 0:512], E8[0:8, e * 128:(e + 1) * 128],
                c_em[0:8, tb * 512:(tb + 1) * 512], start=True, stop=True),
              reads=[E8, c_em], writes=[ps_c])
            E("scalar", lambda e2: e2.activation(c_bc.ap, ps_c.ap[:, 0:512], AF.Copy,
                                                 bias=0.0, scale=1.0),
              reads=[ps_c], writes=[c_bc])
            for hc in range(NHC):
                wb1, wb3 = w1hc[hc % 2], w3hc[hc % 2]
                E("sync", lambda e2, wb=wb1, s=s, hc=hc: e2.dma_start(
                    out=wb.ap, in_=w1t_h.ap[s].rearrange("(j p) h -> p j h", p=128)[:, :, hc * 128:(hc + 1) * 128]),
                  reads=[w1t_h], writes=[wb1], dma=True)
                E("gpsimd", lambda e2, wb=wb3, s=s, hc=hc: e2.dma_start(
                    out=wb.ap, in_=w3t_h.ap[s].rearrange("(j p) h -> p j h", p=128)[:, :, hc * 128:(hc + 1) * 128]),
                  reads=[w3t_h], writes=[wb3], dma=True)
                ph1 = ps_h1 if hc % 2 == 0 else ps_y
                ph2 = ps_h2 if hc % 2 == 0 else ps_a
                sgb = sg if hc % 2 == 0 else sg2
                p.emit_group("tensor", [
                    (lambda e2, wb=wb1, j=j, tb=tb, ph=ph1: e2.matmul(
                        ph.ap[:, 0:512], wb[:, j * 128:(j + 1) * 128],
                        x1fm[:, j * TK + tb * 512: j * TK + (tb + 1) * 512],
                        start=(j == 0), stop=(j == NDM - 1)))
                    for j in range(NDM)], reads=[wb1, x1fm], writes=[ph1])
                p.emit_group("tensor", [
                    (lambda e2, wb=wb3, j=j, tb=tb, ph=ph2: e2.matmul(
                        ph.ap[:, 0:512], wb[:, j * 128:(j + 1) * 128],
                        x1fm[:, j * TK + tb * 512: j * TK + (tb + 1) * 512],
                        start=(j == 0), stop=(j == NDM - 1)))
                    for j in range(NDM)], reads=[wb3, x1fm], writes=[ph2])
                E("scalar", lambda e2, ph=ph1, sgb=sgb: e2.activation(
                    sgb.ap, ph.ap[:, 0:512], AF.Silu, bias=0.0, scale=1.0),
                  reads=[ph1], writes=[sgb])
                E("vector", lambda e2, hc=hc, ph=ph2, sgb=sgb: e2.tensor_mul(
                    hg[:, hc * 512:(hc + 1) * 512], sgb.ap, ph.ap[:, 0:512]),
                  reads=[sgb, ph2], writes=[hg])
            for dmh in range(2):
                ybanks = [ps_y, ps_a, ps_h1, ps_h2]
                for hc in range(NHC):
                    wb2 = w2hc[hc % 2]
                    E("sync" if hc % 2 == 0 else "gpsimd",
                      lambda e2, wb=wb2, s=s, hc=hc, dmh=dmh: e2.dma_start(
                        out=wb.ap[:, 0:512],
                        in_=w2t_h.ap[s][hc * 128:(hc + 1) * 128, dmh * 512:(dmh + 1) * 512]),
                      reads=[w2t_h], writes=[wb2], dma=True)
                    p.emit_group("tensor", [
                        (lambda e2, wb=wb2, hc=hc, dmc=dmc, yb=ybanks[dmc]: e2.matmul(
                            yb.ap[:, 0:512], wb[:, dmc * 128:(dmc + 1) * 128],
                            hg[:, hc * 512:(hc + 1) * 512],
                            start=(hc == 0), stop=(hc == NHC - 1)))
                        for dmc in range(4)], reads=[wb2, hg],
                        writes=[ybanks[0], ybanks[1], ybanks[2], ybanks[3]])
                for dmc in range(4):
                    dj = dmh * 4 + dmc
                    E("vector", lambda e2, yb=ybanks[dmc]: e2.tensor_mul(
                        ysc.ap, yb.ap[:, 0:512], c_bc.ap),
                      reads=[ybanks[dmc], c_bc], writes=[ysc])
                    E("gpsimd", lambda e2, e=e, dj=dj, tb=tb: e2.dma_start(
                        out=acc_h.ap[dj * 128:(dj + 1) * 128, tb * 512:(tb + 1) * 512],
                        in_=ysc.ap, accum_op=(ALU.bypass if e == 0 else ALU.add)),
                      reads=[ysc], writes=[acc_h], dma=True)

    if between is not None:
        between()
    acc_rd2 = io.get("acc_in2")
    for ti in range(NT):
        E("sync", lambda e, ti=ti: e.dma_start(
            out=x_t.ap,
            in_=acc_rd.ap.rearrange("(c p) t -> p c t", p=128)[:, :, ti * 128:(ti + 1) * 128]),
          reads=[acc_rd], writes=[x_t], dma=True)
        if acc_rd2 is not None:
            E("sync", lambda e, ti=ti: e.dma_start(
                out=x1c.ap.bitcast(F32) if False else hgsq.ap,
                in_=acc_rd2.ap.rearrange("(c p) t -> p c t", p=128)[:, :, ti * 128:(ti + 1) * 128]),
              reads=[acc_rd2], writes=[hgsq], dma=True)
            E("vector", lambda e: e.tensor_add(x_t.ap, x_t.ap, hgsq.ap),
              reads=[x_t, hgsq], writes=[x_t])
        for j in range(NDM):
            E("tensor", lambda e, j=j: e.transpose(
                ps_tr.ap[:, j % 4 * 128:(j % 4 + 1) * 128], x_t[:, j * 128:(j + 1) * 128], iden[:]),
              reads=[x_t, iden], writes=[ps_tr])
            E("vector", lambda e, j=j: e.tensor_copy(
                acc_t[:, j * 128:(j + 1) * 128], ps_tr.ap[:, j % 4 * 128:(j % 4 + 1) * 128]),
              reads=[ps_tr], writes=[acc_t])
        E("sync", lambda e, ti=ti: e.dma_start(out=x1t.ap, in_=x1d_h.ap[ti * 128:(ti + 1) * 128, :]),
          reads=[x1d_h], writes=[x1t], dma=True)
        E("vector", lambda e: e.tensor_add(x2t.ap, x1t.ap, acc_t.ap),
          reads=[x1t, acc_t], writes=[x2t])
        ln_inplace(x2t)
        E("scalar", lambda e, ti=ti: e.dma_start(out=x2_h.ap[ti * 128:(ti + 1) * 128, :], in_=x2t.ap),
          reads=[x2t], writes=[x2_h], dma=True)

    return


def build_postffn0(TK=2048):
    nc = bass.Bass("TRN2", target_bir_lowering=False)
    p = Prog(nc)
    sh = load_shared_consts(p)
    A = PhaseAlloc(p)
    HID = 2816
    io = {}
    io["x"] = p.dram("x", [TK, D], F32, kind="ExternalInput")
    io["h0fm"] = p.dram("h0fm", [D, TK], BF16, kind="ExternalInput")
    io["ow"] = p.dram("ow", [D, D], F32, kind="ExternalInput")
    io["ob"] = p.dram("ob", [1, D], F32, kind="ExternalInput")
    io["gate_w"] = p.dram("gate_w", [8, D], F32, kind="ExternalInput")
    io["w1"] = p.dram("w1", [8, HID, D], F32, kind="ExternalInput")
    io["w3"] = p.dram("w3", [8, HID, D], F32, kind="ExternalInput")
    io["w2"] = p.dram("w2", [8, D, HID], F32, kind="ExternalInput")
    io["x2"] = p.dram("x2", [TK, D], F32, kind="ExternalOutput")
    emit_postffn0(p, A, sh, io, TK=TK)
    return p.finish()



def emit_mixer1(p, A, sh, io, T=4096, NH=8, pref="m1_"):
    C = 128
    NCH = T // C
    FD = 16
    HD = 64
    NF1 = NH * FD
    NFV = NH * HD
    HPG = 4
    NHG = NH // HPG
    RD = 4.0
    C1 = 1.0 / (math.sqrt(2.0) * RD)
    C2 = 1.0 / math.sqrt(2.0)
    RRD = 1.0 / math.sqrt(RD)
    COU = 1.0 / (RD * math.sqrt(2.0))

    sb, psum = A.sb, A.psum
    def dram(name, shape, dt=F32, **kw):
        return p.dram(pref + name, shape, dt, **kw)

    x_h = io["x2"]
    wq_h, wk_h, wv_h = io["Wq"], io["Wk"], io["Wv"]
    y_views = io.get("yfm_views")
    if y_views is None:
        yv = io["yfm"]
        y_views = [yv.view(yv.ap[:, 0:T // 2], "yv0"), yv.view(yv.ap[:, T // 2:T], "yv1")]
    qscr_h = dram("qscr", [128, 128], BF16)
    kscr_h = dram("kscr", [128, 128], BF16)

    x2fm = sb("x2fm", [128, NDM * T], BF16)
    qwS = sb("qwS", [128, NDM * NF1], BF16)
    kwS = sb("kwS", [128, NDM * NF1], BF16)
    vwS = sb("vwS", [128, NDM * NFV], BF16)
    wstg = sb("wstg", [128, 512], F32)
    iden, iden16 = sh["iden"], sh["iden16"]
    onescol16 = sh["onescol16"]
    cst = sh["cst"]
    x_t = sb("x_t", [128, D], F32)
    x_c = sb("x_c", [128, D], BF16)
    stg = sb("stg", [128, 128], BF16)
    qfmH = sb("qfmH", [16, NH * C], BF16)
    kfmH = sb("kfmH", [16, NH * C], BF16)
    q_tm = sb("q_tm", [128, NF1], BF16)
    k_tm = sb("k_tm", [128, NF1], BF16)
    v_tm = sb("v_tm", [128, NFV], BF16)
    qsc = sb("qsc", [128, NF1], BF16)
    ksc = sb("ksc", [128, NF1], BF16)
    QF = sb("QF", [128, NH * 273], BF16)
    KF = sb("KF", [128, NH * 273], BF16)
    qf_fm = sb("qf_fm", [128, NH * 3 * C], BF16)
    Ssb = sb("Ssb", [128, HPG * C], F32)
    Pm = sb("Pm", [128, HPG * C], BF16)
    state = sb("state", [128, NH * 3 * 65], F32)
    state16 = sb("state16", [128, NH * 3 * 65], BF16)
    tot = sb("tot", [128, 65], F32)
    dsc = sb("dsc", [128, 2], F32)
    y_tm = sb("y_tm", [128, NFV], BF16)
    y_fm = sb("y_fm", [128, NFV], BF16)

    ps_tr = psum("ps_tr", [128, 512], F32)
    ps_p = psum("ps_p", [128, 512], F32)
    ps_u = psum("ps_u", [128, HPG * C], F32)
    ps_o = psum("ps_o", [128, 512], F32)
    ps_ss = psum("ps_ss", [128, 512], F32)

    E = p.emit

    E("vector", lambda e: e.memset(state.ap, 0.0), writes=[state])
    E("vector", lambda e: e.memset(state16.ap, 0.0), writes=[state16])

    for (wh, wS, nf) in ((wq_h, qwS, NF1), (wk_h, kwS, NF1), (wv_h, vwS, NFV)):
        for j in range(NDM):
            for c0 in range(0, nf, 512):
                cw = min(512, nf - c0)
                E("sync", lambda e, wh=wh, j=j, c0=c0, cw=cw: e.dma_start(
                    out=wstg[:, 0:cw], in_=wh.ap[j * 128:(j + 1) * 128, c0:c0 + cw]),
                  reads=[wh], writes=[wstg], dma=True)
                E("vector", lambda e, wS=wS, j=j, c0=c0, cw=cw, nf=nf: e.tensor_copy(
                    wS[:, j * nf + c0: j * nf + c0 + cw], wstg[:, 0:cw]),
                  reads=[wstg], writes=[wS])

    for i in range(NCH):
        E("sync", lambda e, i=i: e.dma_start(out=x_t.ap, in_=x_h.ap[i * 128:(i + 1) * 128, :]),
          reads=[x_h], writes=[x_t], dma=True)
        E("vector", lambda e: e.tensor_copy(x_c.ap, x_t.ap), reads=[x_t], writes=[x_c])
        for j in range(NDM):
            E("tensor", lambda e, j=j: e.transpose(
                ps_tr.ap.bitcast(BF16)[:, (j % 4) * 128:(j % 4 + 1) * 128],
                x_c[:, j * 128:(j + 1) * 128], iden16[:]),
              reads=[x_c, iden16], writes=[ps_tr])
            E("vector", lambda e, j=j, i=i: e.tensor_copy(
                x2fm[:, j * T + i * 128: j * T + (i + 1) * 128],
                ps_tr.ap.bitcast(BF16)[:, (j % 4) * 128:(j % 4 + 1) * 128]),
              reads=[ps_tr], writes=[x2fm])

    for ch in range(NCH):
        for (wS, scr, fmH) in ((qwS, qscr_h, qfmH), (kwS, kscr_h, kfmH)):
            for j in range(NDM):
                E("tensor", lambda e, wS=wS, j=j, ch=ch: e.matmul(
                    ps_p.ap[:, 0:128], wS[:, j * NF1:(j + 1) * NF1],
                    x2fm[:, j * T + ch * C: j * T + (ch + 1) * C],
                    start=(j == 0), stop=(j == NDM - 1)),
                  reads=[wS, x2fm], writes=[ps_p])
            E("vector", lambda e: e.tensor_copy(stg.ap, ps_p.ap.bitcast(BF16)[:, 0:128]) if False else
              e.tensor_copy(stg.ap, ps_p.ap[:, 0:128]),
              reads=[ps_p], writes=[stg])
            E("scalar", lambda e, scr=scr: e.dma_start(out=scr.ap, in_=stg.ap),
              reads=[stg], writes=[scr], dma=True)
            E("sync", lambda e, scr=scr, fmH=fmH: e.dma_start(
                out=fmH.ap[0:16, :].rearrange("d (h t) -> d h t", t=C),
                in_=scr.ap.rearrange("(h d) t -> d h t", d=16)),
              reads=[scr], writes=[fmH], dma=True)
        for (wS, dtm, nf) in ((qwS, q_tm, NF1), (kwS, k_tm, NF1), (vwS, v_tm, NFV)):
            for j in range(NDM):
                E("tensor", lambda e, wS=wS, j=j, ch=ch, nf=nf: e.matmul(
                    ps_p.ap[:, 0:nf], x2fm[:, j * T + ch * C: j * T + (ch + 1) * C],
                    wS[:, j * nf:(j + 1) * nf],
                    start=(j == 0), stop=(j == NDM - 1)),
                  reads=[wS, x2fm], writes=[ps_p])
            E("vector", lambda e, dtm=dtm, nf=nf: e.tensor_copy(dtm.ap[:, 0:nf], ps_p.ap[:, 0:nf]),
              reads=[ps_p], writes=[dtm])
        for (src_tm, ssc, FT) in ((q_tm, qsc, QF), (k_tm, ksc, KF)):
            E("vector", lambda e, s=src_tm, d=ssc: e.tensor_scalar_mul(d.ap, s.ap, COU),
              reads=[src_tm], writes=[ssc])
            E("vector", lambda e, FT=FT: e.memset(
                FT.ap.rearrange("p (h f) -> p h f", f=273)[:, :, 0:1], 1.0),
              writes=[FT])
            E("vector", lambda e, s=src_tm, FT=FT: e.tensor_scalar_mul(
                FT.ap.rearrange("p (h f) -> p h f", f=273)[:, :, 1:17],
                s.ap.rearrange("p (h f) -> p h f", f=FD), RRD),
              reads=[src_tm], writes=[FT])
            E("vector", lambda e, s=src_tm, sc=ssc, FT=FT: e.tensor_tensor(
                FT.ap.rearrange("p (h f) -> p h f", f=273)[:, :, 17:273]
                    .rearrange("p h (a b) -> p h a b", b=FD),
                sc.ap.rearrange("p (h f) -> p h f", f=FD).unsqueeze(3).broadcast_to([128, NH, FD, FD]),
                s.ap.rearrange("p (h f) -> p h f", f=FD).unsqueeze(2).broadcast_to([128, NH, FD, FD]),
                ALU.mult),
              reads=[src_tm, ssc], writes=[FT])
        for h in range(NH):
            for fc3 in range(3):
                rows = 128 if fc3 < 2 else 17
                E("tensor", lambda e, h=h, fc3=fc3, rows=rows: e.transpose(
                    ps_tr.ap.bitcast(BF16)[0:rows, 512 + 0:512 + 128] if False else
                    ps_tr.ap.bitcast(BF16)[0:rows, 0:128],
                    QF[:, h * 273 + fc3 * 128: h * 273 + fc3 * 128 + rows], iden16[:]),
                  reads=[QF, iden16], writes=[ps_tr])
                E("vector", lambda e, h=h, fc3=fc3, rows=rows: e.tensor_copy(
                    qf_fm[0:rows, (h * 3 + fc3) * C:(h * 3 + fc3) * C + 128],
                    ps_tr.ap.bitcast(BF16)[0:rows, 0:128]),
                  reads=[ps_tr], writes=[qf_fm])
        for hg in range(NHG):
            h0 = hg * HPG
            for bi in range(HPG):
                h = h0 + bi
                E("tensor", lambda e, h=h, bi=bi: e.matmul(
                    ps_u.ap[:, bi * C:(bi + 1) * C],
                    kfmH[0:16, h * C:(h + 1) * C], qfmH[0:16, h * C:(h + 1) * C],
                    start=True, stop=True),
                  reads=[kfmH, qfmH], writes=[ps_u])
            E("scalar", lambda e: e.activation(Ssb.ap, ps_u.ap, AF.Square, bias=cst[:, 1:2], scale=C1),
              reads=[ps_u, cst], writes=[Ssb])
            E("vector", lambda e: e.tensor_scalar(Pm.ap, Ssb.ap, 0.5, None, ALU.add),
              reads=[Ssb], writes=[Pm])
            E("gpsimd", lambda e: e.affine_select(
                Pm.ap, Pm.ap, [[0, HPG], [1, C]], ALU.is_ge, 0.0,
                base=0, channel_multiplier=-1),
              reads=[Pm], writes=[Pm])
            for bi in range(HPG):
                h = h0 + bi
                E("tensor", lambda e, h=h, bi=bi: e.matmul(
                    ps_o.ap[:, 0:HD], Pm[:, bi * C:(bi + 1) * C],
                    v_tm[:, h * HD:(h + 1) * HD], start=True, stop=True),
                  reads=[Pm, v_tm], writes=[ps_o])
                E("tensor", lambda e, bi=bi: e.matmul(
                    ps_o.ap[:, HD:HD + 1], Pm[:, bi * C:(bi + 1) * C], onescol16[:],
                    start=True, stop=True),
                  reads=[Pm, onescol16], writes=[ps_o])
                for fc3 in range(3):
                    rows = 128 if fc3 < 2 else 17
                    E("tensor", lambda e, h=h, fc3=fc3, rows=rows: e.matmul(
                        ps_o.ap[:, 128:128 + HD + 1],
                        qf_fm[0:rows, (h * 3 + fc3) * C:(h * 3 + fc3 + 1) * C],
                        state16[0:rows, (h * 3 + fc3) * 65:(h * 3 + fc3 + 1) * 65],
                        start=(fc3 == 0), stop=(fc3 == 2)),
                      reads=[qf_fm, state16], writes=[ps_o])
                E("scalar", lambda e: e.activation(tot.ap, ps_o.ap[:, 0:65], AF.Copy,
                                                   bias=0.0, scale=1.0),
                  reads=[ps_o], writes=[tot])
                E("vector", lambda e: e.tensor_add(tot.ap, tot.ap, ps_o.ap[:, 128:128 + 65]),
                  reads=[tot, ps_o], writes=[tot])
                E("vector", lambda e: e.tensor_scalar(dsc[:, 0:1], tot[:, HD:HD + 1],
                                                      cst[:, 2:3], None, ALU.add),
                  reads=[tot, cst], writes=[dsc])
                E("vector", lambda e: e.reciprocal(dsc[:, 1:2], dsc[:, 0:1]),
                  reads=[dsc], writes=[dsc])
                E("vector", lambda e, h=h: e.tensor_scalar(
                    y_tm[:, h * HD:(h + 1) * HD], tot[:, 0:HD], dsc[:, 1:2], None, ALU.mult),
                  reads=[tot, dsc], writes=[y_tm])
                for fc3 in range(3):
                    rows = 128 if fc3 < 2 else 17
                    E("tensor", lambda e, h=h, fc3=fc3, rows=rows: e.matmul(
                        ps_ss.ap[0:rows, 0:HD],
                        KF[:, h * 273 + fc3 * 128: h * 273 + fc3 * 128 + rows],
                        v_tm[:, h * HD:(h + 1) * HD], start=True, stop=True),
                      reads=[KF, v_tm], writes=[ps_ss])
                    E("tensor", lambda e, h=h, fc3=fc3, rows=rows: e.matmul(
                        ps_ss.ap[0:rows, HD:HD + 1],
                        KF[:, h * 273 + fc3 * 128: h * 273 + fc3 * 128 + rows],
                        onescol16[:], start=True, stop=True),
                      reads=[KF, onescol16], writes=[ps_ss])
                    E("vector", lambda e, h=h, fc3=fc3, rows=rows: e.tensor_add(
                        state[0:rows, (h * 3 + fc3) * 65:(h * 3 + fc3 + 1) * 65],
                        state[0:rows, (h * 3 + fc3) * 65:(h * 3 + fc3 + 1) * 65],
                        ps_ss.ap[0:rows, 0:HD + 1]),
                      reads=[state, ps_ss], writes=[state])
                    E("vector", lambda e, h=h, fc3=fc3, rows=rows: e.tensor_copy(
                        state16[0:rows, (h * 3 + fc3) * 65:(h * 3 + fc3 + 1) * 65],
                        state[0:rows, (h * 3 + fc3) * 65:(h * 3 + fc3 + 1) * 65]),
                      reads=[state], writes=[state16])
        for fc2 in range(NFV // 128):
            E("tensor", lambda e, fc2=fc2: e.transpose(
                ps_tr.ap.bitcast(BF16)[:, fc2 * 128:(fc2 + 1) * 128],
                y_tm[:, fc2 * 128:(fc2 + 1) * 128], iden16[:]),
              reads=[y_tm, iden16], writes=[ps_tr])
            E("vector", lambda e, fc2=fc2: e.tensor_copy(
                y_fm[:, fc2 * 128:(fc2 + 1) * 128],
                ps_tr.ap.bitcast(BF16)[:, fc2 * 128:(fc2 + 1) * 128]),
              reads=[ps_tr], writes=[y_fm])
        yv = y_views[ch // (NCH // 2)]
        yoff = (ch * C) % (T // 2)
        E("scalar", lambda e, yv=yv, yoff=yoff: e.dma_start(
            out=yv.ap.rearrange("(fc p) t -> p fc t", p=128)[:, :, yoff:yoff + C],
            in_=y_fm.ap.rearrange("p (fc t) -> p fc t", t=C)),
          reads=[y_fm], writes=[yv], dma=True)

    return


def build_mixer1(T=4096, NH=8):
    nc = bass.Bass("TRN2", target_bir_lowering=False)
    p = Prog(nc)
    sh = load_shared_consts(p)
    A = PhaseAlloc(p)
    NF1, NFV = NH * 16, NH * 64
    io = {}
    io["x2"] = p.dram("x2", [T, D], F32, kind="ExternalInput")
    io["Wq"] = p.dram("Wq", [D, NF1], F32, kind="ExternalInput")
    io["Wk"] = p.dram("Wk", [D, NF1], F32, kind="ExternalInput")
    io["Wv"] = p.dram("Wv", [D, NFV], F32, kind="ExternalInput")
    io["yfm"] = p.dram("yfm", [NFV, T], BF16, kind="ExternalOutput")
    emit_mixer1(p, A, sh, io, T=T, NH=NH)
    return p.finish()



def emit_postffn1(p, A, sh, io, TK=2048, NE=8, pref="p1_", between=None):
    NT = TK // 128
    NTB = TK // 512
    HID = 4096
    NHC = HID // 128

    sb, psum = A.sb, A.psum
    def dram(name, shape, dt=F32, **kw):
        return p.dram(pref + name, shape, dt, **kw)

    x_h = io["x2"]
    y_h = io["yfm"]
    wo_h, wg_h, bg_h = io["Wo"], io["Wg"], io["bg"]
    w0_h, b0_h = io["W0"], io["b0"]
    w1_h, b1_h = io["W1"], io["b1"]
    w2_h, b2_h = io["W2"], io["b2"]
    x4_h = io["x4"]
    x3d_h = dram("x3d", [TK, D], F32)
    acc_h = io.get("acc_out") or dram("accd", [D, TK], F32)
    acc_rd = io.get("acc_in") or acc_h
    w0c_h = dram("w0c", [2, D, HID], BF16)
    w1c_h = dram("w1c", [2, D, HID], BF16)
    w2c_h = dram("w2c", [2, HID, D], BF16)

    woS = sb("woS", [128, NDM * D], BF16)
    gwT = sb("gwT", [128, NDM * 8], BF16)
    bgr16 = sb("bgr16", [1, 8], BF16)
    brow = sb("brow", [1, HID // 2], F32)
    iden, iden16 = sh["iden"], sh["iden16"]
    onesrow16, E8 = sh["onesrow16"], sh["E8"]
    cst = sh["cst"]
    wstg = sb("wstg", [128, 1024], F32)
    w0t = [sb(f"w0t{i}", [128, NDM * 128], BF16) for i in range(2)]
    w1t = [sb(f"w1t{i}", [128, NDM * 128], BF16) for i in range(2)]
    w2t = [sb(f"w2t{i}", [128, 512], BF16) for i in range(2)]
    cstg = sb("cstg", [128, 1024], BF16)
    b0c = sb("b0c", [128, NHC], F32)
    b1c = sb("b1c", [128, NHC], F32)
    b2c = sb("b2c", [128, 8], F32)
    x_t = sb("x_t", [128, D], F32)
    _yf0 = sb("yf_t0", [128, NDM * 128], BF16)
    yf_t = [_yf0, _yf0]
    x3t = sb("x3t", [128, D], F32)
    x3c = sb("x3c", [128, D], BF16)
    x3fm = sb("x3fm", [128, NDM * TK], BF16)
    stat = sb("stat", [128, 8], F32)
    hgsq = sb("hgsq", [128, D], F32)
    lg = sb("lg", [128, 8], F32)
    cwk = sb("cwk", [128, 24], F32)
    c_tm = sb("c_tm", [128, 8], F32)
    c_em = sb("c_em", [8, TK], F32)
    c_bc = sb("c_bc", [128, 512], F32)
    sg = sb("sg", [128, 512], F32)
    c_tm2 = sb("c_tm2", [128, 512], F32)
    hg = sb("hg", [128, NHC * 512], BF16)
    ysc = sb("ysc", [128, 512], F32)
    acc_t = sb("acc_t", [128, D], F32)
    x4t = sb("x4t", [128, D], F32)

    ps_tr = psum("ps_tr", [128, 512], F32)
    ps_a = psum("ps_a", [128, 512], F32)
    ps_b = psum("ps_b", [128, 512], F32)
    ps_h1 = psum("ps_h1", [128, 512], F32)
    ps_h2 = psum("ps_h2", [128, 512], F32)
    ps_y = psum("ps_y", [128, 512], F32)
    ps_c = psum("ps_c", [128, 512], F32)

    E = p.emit

    E("sync", lambda e: e.dma_start(out=brow[0:1, 0:8], in_=bg_h.ap), reads=[bg_h], writes=[brow], dma=True)
    E("vector", lambda e: e.tensor_copy(bgr16.ap, brow[0:1, 0:8]), reads=[brow], writes=[bgr16])

    for j in range(NDM):
        E("sync", lambda e, j=j: e.dma_start(out=wstg[:, 0:D], in_=wo_h.ap[j * 128:(j + 1) * 128, :]),
          reads=[wo_h], writes=[wstg], dma=True)
        E("vector", lambda e, j=j: e.tensor_copy(woS[:, j * D:(j + 1) * D], wstg[:, 0:D]),
          reads=[wstg], writes=[woS])
        E("sync", lambda e, j=j: e.dma_start(out=wstg[:, 0:8], in_=wg_h.ap[j * 128:(j + 1) * 128, :]),
          reads=[wg_h], writes=[wstg], dma=True)
        E("vector", lambda e, j=j: e.tensor_copy(gwT[:, j * 8:(j + 1) * 8], wstg[:, 0:8]),
          reads=[wstg], writes=[gwT])

    def ln_inplace(buf):
        E("vector", lambda e: e.reduce_sum(stat[:, 0:1], buf[:, 0:D], axis=AX.X),
          reads=[buf], writes=[stat])
        E("vector", lambda e: e.tensor_mul(hgsq[:, 0:D], buf[:, 0:D], buf[:, 0:D]),
          reads=[buf], writes=[hgsq])
        E("vector", lambda e: e.reduce_sum(stat[:, 1:2], hgsq[:, 0:D], axis=AX.X),
          reads=[hgsq], writes=[stat])
        E("vector", lambda e: e.tensor_scalar_mul(stat[:, 2:3], stat[:, 0:1], 1.0 / D),
          reads=[stat], writes=[stat])
        E("vector", lambda e: e.tensor_mul(stat[:, 3:4], stat[:, 2:3], stat[:, 2:3]),
          reads=[stat], writes=[stat])
        E("vector", lambda e: e.scalar_tensor_tensor(
            stat[:, 4:5], stat[:, 1:2], 1.0 / D, stat[:, 3:4], ALU.mult, ALU.subtract),
          reads=[stat], writes=[stat])
        E("scalar", lambda e: e.activation(stat[:, 5:6], stat[:, 4:5], AF.Ln, bias=cst[:, 0:1], scale=1.0),
          reads=[stat, cst], writes=[stat])
        E("scalar", lambda e: e.activation(stat[:, 6:7], stat[:, 5:6], AF.Exp, bias=0.0, scale=-0.5),
          reads=[stat], writes=[stat])
        E("vector", lambda e: e.tensor_scalar(
            buf[:, 0:D], buf[:, 0:D], stat[:, 2:3], stat[:, 6:7], ALU.subtract, ALU.mult),
          reads=[buf, stat], writes=[buf])

    for ti in range(NT):
        yb = yf_t[ti % 2]
        E("sync", lambda e, yb=yb, ti=ti: e.dma_start(
            out=yb.ap, in_=y_h.ap.rearrange("(c p) t -> p c t", p=128)[:, :, ti * 128:(ti + 1) * 128]),
          reads=[y_h], writes=[yb], dma=True)
        E("sync", lambda e, ti=ti: e.dma_start(out=x_t.ap, in_=x_h.ap[ti * 128:(ti + 1) * 128, :]),
          reads=[x_h], writes=[x_t], dma=True)
        for dmb in range(2):
            for hf in range(NDM):
                E("tensor", lambda e, yb=yb, hf=hf, dmb=dmb: e.matmul(
                    ps_a.ap[:, 0:512], yb[:, hf * 128:(hf + 1) * 128],
                    woS[:, hf * D + dmb * 512: hf * D + dmb * 512 + 512],
                    start=(hf == 0), stop=(hf == NDM - 1)),
                  reads=[yb, woS], writes=[ps_a])
            E("vector", lambda e, dmb=dmb: e.tensor_add(
                x3t[:, dmb * 512:(dmb + 1) * 512], x_t[:, dmb * 512:(dmb + 1) * 512],
                ps_a.ap[:, 0:512]),
              reads=[x_t, ps_a], writes=[x3t])
        ln_inplace(x3t)
        E("scalar", lambda e, ti=ti: e.dma_start(out=x3d_h.ap[ti * 128:(ti + 1) * 128, :], in_=x3t.ap),
          reads=[x3t], writes=[x3d_h], dma=True)
        E("vector", lambda e: e.tensor_copy(x3c.ap, x3t.ap), reads=[x3t], writes=[x3c])
        for j in range(NDM):
            E("tensor", lambda e, j=j: e.transpose(
                ps_tr.ap.bitcast(BF16)[:, (j % 4) * 128:(j % 4 + 1) * 128],
                x3c[:, j * 128:(j + 1) * 128], iden16[:]),
              reads=[x3c, iden16], writes=[ps_tr])
            E("vector", lambda e, j=j, ti=ti: e.tensor_copy(
                x3fm[:, j * TK + ti * 128: j * TK + (ti + 1) * 128],
                ps_tr.ap.bitcast(BF16)[:, (j % 4) * 128:(j % 4 + 1) * 128]),
              reads=[ps_tr], writes=[x3fm])
        for j in range(NDM):
            E("tensor", lambda e, j=j, ti=ti: e.matmul(
                ps_b.ap[:, 0:8], x3fm[:, j * TK + ti * 128: j * TK + (ti + 1) * 128],
                gwT[:, j * 8:(j + 1) * 8], start=(j == 0), stop=False),
              reads=[x3fm, gwT], writes=[ps_b])
        E("tensor", lambda e: e.matmul(ps_b.ap[:, 0:8], onesrow16[0:1, :], bgr16[0:1, 0:8],
                                       start=False, stop=True),
          reads=[onesrow16, bgr16], writes=[ps_b])
        E("vector", lambda e: e.tensor_copy(lg.ap, ps_b.ap[:, 0:8]), reads=[ps_b], writes=[lg])
        E("vector", lambda e: e.reduce_max(cwk[:, 0:1], lg.ap, axis=AX.X), reads=[lg], writes=[cwk])
        E("vector", lambda e: e.tensor_scalar(cwk[:, 8:16], lg.ap, cwk[:, 0:1], None, ALU.subtract),
          reads=[lg, cwk], writes=[cwk])
        E("vector", lambda e: e.tensor_scalar(cwk[:, 16:24], cwk[:, 8:16], 0.0, None, ALU.is_equal),
          reads=[cwk], writes=[cwk])
        E("vector", lambda e: e.scalar_tensor_tensor(
            c_tm.ap, cwk[:, 16:24], -1e30, lg.ap, ALU.mult, ALU.add),
          reads=[cwk, lg], writes=[c_tm])
        E("vector", lambda e: e.reduce_max(cwk[:, 1:2], c_tm.ap, axis=AX.X), reads=[c_tm], writes=[cwk])
        E("vector", lambda e: e.tensor_scalar(cwk[:, 2:3], cwk[:, 1:2], cwk[:, 0:1], None, ALU.subtract),
          reads=[cwk], writes=[cwk])
        E("scalar", lambda e: e.activation(cwk[:, 3:4], cwk[:, 2:3], AF.Exp, bias=0.0, scale=1.0),
          reads=[cwk], writes=[cwk])
        E("vector", lambda e: e.tensor_scalar(cwk[:, 4:5], cwk[:, 3:4], 1.0, None, ALU.add),
          reads=[cwk], writes=[cwk])
        E("vector", lambda e: e.reciprocal(cwk[:, 5:6], cwk[:, 4:5]), reads=[cwk], writes=[cwk])
        E("vector", lambda e: e.tensor_mul(cwk[:, 6:7], cwk[:, 3:4], cwk[:, 5:6]),
          reads=[cwk], writes=[cwk])
        E("vector", lambda e: e.tensor_scalar(cwk[:, 8:16], c_tm.ap, cwk[:, 1:2], None, ALU.is_equal),
          reads=[c_tm, cwk], writes=[cwk])
        E("vector", lambda e: e.tensor_scalar(c_tm.ap, cwk[:, 8:16], cwk[:, 6:7], None, ALU.mult),
          reads=[cwk], writes=[c_tm])
        E("vector", lambda e: e.scalar_tensor_tensor(
            c_tm.ap, cwk[:, 16:24], cwk[:, 5:6], c_tm.ap, ALU.mult, ALU.add),
          reads=[cwk, c_tm], writes=[c_tm])
        E("tensor", lambda e: e.transpose(ps_c.ap[0:8, 0:128], c_tm.ap, iden[:]),
          reads=[c_tm, iden], writes=[ps_c])
        E("vector", lambda e, ti=ti: e.tensor_copy(c_em[0:8, ti * 128:(ti + 1) * 128], ps_c.ap[0:8, 0:128]),
          reads=[ps_c], writes=[c_em])

    def cast_expert(e):
        s = e % 2
        for (wh, wc, rows, cols) in ((w0_h, w0c_h, D, HID), (w1_h, w1c_h, D, HID),
                                     (w2_h, w2c_h, HID, D)):
            for r0 in range(0, rows, 128):
                for c0 in range(0, cols, 1024):
                    cw = min(1024, cols - c0)
                    E("scalar", lambda e2, wh=wh, e=e, r0=r0, c0=c0, cw=cw: e2.dma_start(
                        out=wstg[:, 0:cw], in_=wh.ap[e][r0:r0 + 128, c0:c0 + cw]),
                      reads=[wh], writes=[wstg], dma=True)
                    E("vector", lambda e2, cw=cw: e2.tensor_copy(
                        w0t[0].ap[:, 0:cw] if cw <= 1024 else wstg.ap.bitcast(BF16)[:, 0:cw],
                        wstg[:, 0:cw]) if False else e2.tensor_copy(cstg[:, 0:cw], wstg[:, 0:cw]),
                      reads=[wstg], writes=[cstg])
                    E("scalar", lambda e2, wc=wc, s=s, r0=r0, c0=c0, cw=cw: e2.dma_start(
                        out=wc.ap[s][r0:r0 + 128, c0:c0 + cw], in_=cstg[:, 0:cw]),
                      reads=[cstg], writes=[wc], dma=True)

    cast_expert(0)
    for e in range(NE):
        for (bh, bc, n) in ((b0_h, b0c, NHC), (b1_h, b1c, NHC), (b2_h, b2c, 8)):
            for half0 in range(0, n, 16):
                nn = min(16, n - half0)
                E("sync", lambda e2, bh=bh, e=e, half0=half0, nn=nn: e2.dma_start(
                    out=brow[0:1, 0:nn * 128], in_=bh.ap[e:e + 1, half0 * 128:(half0 + nn) * 128]),
                  reads=[bh], writes=[brow], dma=True)
                for j in range(nn):
                    E("tensor", lambda e2, j=j: e2.transpose(
                        ps_c.ap[:, 128 + j % 4:128 + j % 4 + 1],
                        brow[0:1, j * 128:(j + 1) * 128], iden[0:1, 0:1]),
                      reads=[brow, iden], writes=[ps_c])
                    E("vector", lambda e2, bc=bc, j=j, half0=half0: e2.tensor_copy(
                        bc[:, half0 + j:half0 + j + 1], ps_c.ap[:, 128 + j % 4:128 + j % 4 + 1]),
                      reads=[ps_c], writes=[bc])
        if e < NE - 1:
            cast_expert(e + 1)
        s = e % 2
        for tb in range(NTB):
            E("tensor", lambda e2, e=e, tb=tb: e2.matmul(
                ps_c.ap[:, 0:512], E8[0:8, e * 128:(e + 1) * 128],
                c_em[0:8, tb * 512:(tb + 1) * 512], start=True, stop=True),
              reads=[E8, c_em], writes=[ps_c])
            E("scalar", lambda e2: e2.activation(c_bc.ap, ps_c.ap[:, 0:512], AF.Copy,
                                                 bias=0.0, scale=1.0),
              reads=[ps_c], writes=[c_bc])
            for hc in range(NHC):
                wb0, wb1 = w0t[hc % 2], w1t[hc % 2]
                E("sync", lambda e2, s=s, hc=hc, wb=wb0: e2.dma_start(
                    out=wb.ap,
                    in_=w0c_h.ap[s].rearrange("(j p) h -> p j h", p=128)[:, :, hc * 128:(hc + 1) * 128]),
                  reads=[w0c_h], writes=[wb0], dma=True)
                E("gpsimd", lambda e2, s=s, hc=hc, wb=wb1: e2.dma_start(
                    out=wb.ap,
                    in_=w1c_h.ap[s].rearrange("(j p) h -> p j h", p=128)[:, :, hc * 128:(hc + 1) * 128]),
                  reads=[w1c_h], writes=[wb1], dma=True)
                ph1 = ps_h1 if hc % 2 == 0 else ps_y
                ph2 = ps_h2 if hc % 2 == 0 else ps_a
                p.emit_group("tensor", [
                    (lambda e2, wb=wb0, j=j, tb=tb, ph=ph1: e2.matmul(
                        ph.ap[:, 0:512], wb[:, j * 128:(j + 1) * 128],
                        x3fm[:, j * TK + tb * 512: j * TK + (tb + 1) * 512],
                        start=(j == 0), stop=(j == NDM - 1)))
                    for j in range(NDM)], reads=[wb0, x3fm], writes=[ph1])
                p.emit_group("tensor", [
                    (lambda e2, wb=wb1, j=j, tb=tb, ph=ph2: e2.matmul(
                        ph.ap[:, 0:512], wb[:, j * 128:(j + 1) * 128],
                        x3fm[:, j * TK + tb * 512: j * TK + (tb + 1) * 512],
                        start=(j == 0), stop=(j == NDM - 1)))
                    for j in range(NDM)], reads=[wb1, x3fm], writes=[ph2])
                sgb = sg if hc % 2 == 0 else c_tm2
                E("scalar", lambda e2, hc=hc, ph=ph2, sgb=sgb: e2.activation(
                    sgb.ap[:, 0:512], ph.ap[:, 0:512], AF.Silu,
                    bias=b1c[:, hc:hc + 1], scale=1.0),
                  reads=[ph2, b1c], writes=[sgb])
                E("vector", lambda e2, hc=hc, ph=ph1, sgb=sgb: e2.scalar_tensor_tensor(
                    hg[:, hc * 512:(hc + 1) * 512], ph.ap[:, 0:512], b0c[:, hc:hc + 1],
                    sgb.ap[:, 0:512], ALU.add, ALU.mult),
                  reads=[ph1, b0c, sgb], writes=[hg])
            for dmh in range(2):
                ybanks = [ps_y, ps_a, ps_h1, ps_h2]
                for hc in range(NHC):
                    wb2 = w2t[hc % 2]
                    E("sync" if hc % 2 == 0 else "gpsimd",
                      lambda e2, s=s, hc=hc, dmh=dmh, wb=wb2: e2.dma_start(
                        out=wb.ap,
                        in_=w2c_h.ap[s][hc * 128:(hc + 1) * 128, dmh * 512:(dmh + 1) * 512]),
                      reads=[w2c_h], writes=[wb2], dma=True)
                    p.emit_group("tensor", [
                        (lambda e2, wb=wb2, hc=hc, dmc=dmc, yb=ybanks[dmc]: e2.matmul(
                            yb.ap[:, 0:512], wb[:, dmc * 128:(dmc + 1) * 128],
                            hg[:, hc * 512:(hc + 1) * 512],
                            start=(hc == 0), stop=(hc == NHC - 1)))
                        for dmc in range(4)], reads=[wb2, hg],
                        writes=[ybanks[0], ybanks[1], ybanks[2], ybanks[3]])
                for dmc in range(4):
                    dj = dmh * 4 + dmc
                    E("vector", lambda e2, yb=ybanks[dmc], dj=dj: e2.scalar_tensor_tensor(
                        ysc.ap, yb.ap[:, 0:512], b2c[:, dj:dj + 1], c_bc.ap, ALU.add, ALU.mult),
                      reads=[ybanks[dmc], b2c, c_bc], writes=[ysc])
                    E("gpsimd", lambda e2, e=e, dj=dj, tb=tb: e2.dma_start(
                        out=acc_h.ap[dj * 128:(dj + 1) * 128, tb * 512:(tb + 1) * 512],
                        in_=ysc.ap, accum_op=(ALU.bypass if e == 0 else ALU.add)),
                      reads=[ysc], writes=[acc_h], dma=True)

    if between is not None:
        between()
    acc_rd2 = io.get("acc_in2")
    for ti in range(NT):
        E("sync", lambda e, ti=ti: e.dma_start(
            out=x_t.ap,
            in_=acc_rd.ap.rearrange("(c p) t -> p c t", p=128)[:, :, ti * 128:(ti + 1) * 128]),
          reads=[acc_rd], writes=[x_t], dma=True)
        if acc_rd2 is not None:
            E("sync", lambda e, ti=ti: e.dma_start(
                out=hgsq.ap,
                in_=acc_rd2.ap.rearrange("(c p) t -> p c t", p=128)[:, :, ti * 128:(ti + 1) * 128]),
              reads=[acc_rd2], writes=[hgsq], dma=True)
            E("vector", lambda e: e.tensor_add(x_t.ap, x_t.ap, hgsq.ap),
              reads=[x_t, hgsq], writes=[x_t])
        for j in range(NDM):
            E("tensor", lambda e, j=j: e.transpose(
                ps_tr.ap[:, (j % 4) * 128:(j % 4 + 1) * 128], x_t[:, j * 128:(j + 1) * 128], iden[:]),
              reads=[x_t, iden], writes=[ps_tr])
            E("vector", lambda e, j=j: e.tensor_copy(
                acc_t[:, j * 128:(j + 1) * 128], ps_tr.ap[:, (j % 4) * 128:(j % 4 + 1) * 128]),
              reads=[ps_tr], writes=[acc_t])
        E("sync", lambda e, ti=ti: e.dma_start(out=x3t.ap, in_=x3d_h.ap[ti * 128:(ti + 1) * 128, :]),
          reads=[x3d_h], writes=[x3t], dma=True)
        E("vector", lambda e: e.tensor_add(x4t.ap, x3t.ap, acc_t.ap),
          reads=[x3t, acc_t], writes=[x4t])
        ln_inplace(x4t)
        E("scalar", lambda e, ti=ti: e.dma_start(out=x4_h.ap[ti * 128:(ti + 1) * 128, :], in_=x4t.ap),
          reads=[x4t], writes=[x4_h], dma=True)

    return


def build_postffn1(TK=2048):
    nc = bass.Bass("TRN2", target_bir_lowering=False)
    p = Prog(nc)
    sh = load_shared_consts(p)
    A = PhaseAlloc(p)
    HID = 4096
    io = {}
    io["x2"] = p.dram("x2", [TK, D], F32, kind="ExternalInput")
    io["yfm"] = p.dram("yfm", [D, TK], BF16, kind="ExternalInput")
    io["Wo"] = p.dram("Wo", [D, D], F32, kind="ExternalInput")
    io["Wg"] = p.dram("Wg", [D, 8], F32, kind="ExternalInput")
    io["bg"] = p.dram("bg", [1, 8], F32, kind="ExternalInput")
    io["W0"] = p.dram("W0", [8, D, HID], F32, kind="ExternalInput")
    io["b0"] = p.dram("b0", [8, HID], F32, kind="ExternalInput")
    io["W1"] = p.dram("W1", [8, D, HID], F32, kind="ExternalInput")
    io["b1"] = p.dram("b1", [8, HID], F32, kind="ExternalInput")
    io["W2"] = p.dram("W2", [8, HID, D], F32, kind="ExternalInput")
    io["b2"] = p.dram("b2", [8, D], F32, kind="ExternalInput")
    io["x4"] = p.dram("x4", [TK, D], F32, kind="ExternalOutput")
    emit_postffn1(p, A, sh, io, TK=TK)
    return p.finish()


def run_mixer0_test():
    pass





def build_fused(T=4096):
    nc = bass.Bass("TRN2", target_bir_lowering=False)
    p = Prog(nc)
    sh = load_shared_consts(p, NH_E32=32)
    A = PhaseAlloc(p)
    E = p.emit
    PAIRS = [[0, 1], [2, 3], [4, 5], [6, 7]]

    ei = lambda n, s, dt=F32: p.dram(n, s, dt, kind="ExternalInput")
    m0 = dict(
        x=ei("x", [T, D]),
        qw=ei("qw", [512, D]), kw=ei("kw", [512, D]), vw=ei("vw", [512, D]),
        iw=ei("iw", [32, D]), fw=ei("fw", [32, D]),
        qb=ei("qb", [1, 512]), kb=ei("kb", [1, 512]), vb=ei("vb", [1, 512]),
        ib=ei("ib", [1, 32]), fb=ei("fb", [1, 32]),
        gamma=ei("gamma", [1, D]), beta=ei("beta", [1, D]),
    )
    p0 = dict(
        ow=ei("ow", [D, D]), ob=ei("ob", [1, D]), gate_w=ei("gate_w", [8, D]),
        w1=ei("w1", [4, 2816, D]), w3=ei("w3", [4, 2816, D]), w2=ei("w2", [4, D, 2816]),
    )
    m1 = dict(Wq=ei("Wq", [D, 128]), Wk=ei("Wk", [D, 128]), Wv=ei("Wv", [D, 512]))
    p1 = dict(
        Wo=ei("Wo", [D, D]), Wg=ei("Wg", [D, 8]), bg=ei("bg", [1, 8]),
        W0=ei("W0", [4, D, 4096]), b0=ei("b0", [4, 4096]),
        W1=ei("W1", [4, D, 4096]), b1=ei("b1", [4, 4096]),
        W2=ei("W2", [4, 4096, D]), b2=ei("b2", [4, D]),
    )
    out_h = p.dram("x4", [T, D], F32, kind="ExternalOutput")

    h0x = p.dram("h0x", [512 * T], BF16)
    h0g = p.dram("h0g", [2, 512 * T], BF16)
    acc0 = p.dram("acc0", [D * T], F32)
    acc0g = p.dram("acc0g", [2, D * T], F32)
    y1x = p.dram("y1x", [512 * T], BF16)
    y1g = p.dram("y1g", [2, 512 * T], BF16)
    acc1 = p.dram("acc1", [D * T], F32)
    acc1g = p.dram("acc1g", [2, D * T], F32)

    m0io = dict(m0)
    m0io["hout"] = h0x.view(h0x.ap.rearrange("(f t) -> f t", t=T), "m0hout")
    emit_mixer0(p, A, sh, m0io, T=T, NH=32, pref="m0_")
    phase_barrier(p)
    A.reset()
    p.emit_cc(lambda e: e.collective_compute(
        "AllGather", ALU.bypass, replica_groups=PAIRS,
        ins=[h0x.ap], outs=[h0g.ap]),
      reads=[h0x], writes=[h0g])

    p0io = dict(p0)
    p0io["x"] = m0["x"]
    p0io["h0fm"] = h0g.view(h0g.ap.rearrange("h (f t) -> (h f) t", t=T), "p0h0")
    p0io["acc_out"] = acc0.view(acc0.ap.rearrange("(d t) -> d t", t=T), "acc0v")
    p0io["acc_in"] = acc0g.view(acc0g.ap[0:1, :].rearrange("o (d t) -> (o d) t", t=T), "acc0g0")
    p0io["acc_in2"] = acc0g.view(acc0g.ap[1:2, :].rearrange("o (d t) -> (o d) t", t=T), "acc0g1")

    def red0():
        phase_barrier(p)
        p.emit_cc(lambda e: e.collective_compute(
            "AllGather", ALU.bypass, replica_groups=PAIRS,
            ins=[acc0.ap], outs=[acc0g.ap]),
          reads=[acc0], writes=[acc0g])

    x2full = p.dram("x2full", [T, D], F32)
    p0io["x2"] = x2full
    emit_postffn0(p, A, sh, p0io, TK=T, NE=4, pref="p0_", between=red0)
    phase_barrier(p)
    A.reset()

    m1io = dict(m1)
    m1io["x2"] = x2full
    m1io["yfm"] = y1x.view(y1x.ap.rearrange("(f t) -> f t", t=T), "m1y")
    emit_mixer1(p, A, sh, m1io, T=T, NH=8, pref="m1_")
    phase_barrier(p)
    A.reset()
    p.emit_cc(lambda e: e.collective_compute(
        "AllGather", ALU.bypass, replica_groups=PAIRS,
        ins=[y1x.ap], outs=[y1g.ap]),
      reads=[y1x], writes=[y1g])

    p1io = dict(p1)
    p1io["x2"] = x2full
    p1io["yfm"] = y1g.view(y1g.ap.rearrange("h (f t) -> (h f) t", t=T), "p1y")
    p1io["acc_out"] = acc1.view(acc1.ap.rearrange("(d t) -> d t", t=T), "acc1v")
    p1io["acc_in"] = acc1g.view(acc1g.ap[0:1, :].rearrange("o (d t) -> (o d) t", t=T), "acc1g0")
    p1io["acc_in2"] = acc1g.view(acc1g.ap[1:2, :].rearrange("o (d t) -> (o d) t", t=T), "acc1g1")

    def red1():
        phase_barrier(p)
        p.emit_cc(lambda e: e.collective_compute(
            "AllGather", ALU.bypass, replica_groups=PAIRS,
            ins=[acc1.ap], outs=[acc1g.ap]),
          reads=[acc1], writes=[acc1g])

    p1io["x4"] = out_h
    emit_postffn1(p, A, sh, p1io, TK=T, NE=4, pref="p1_", between=red1)

    return p.finish()


def kernel_fused(**inputs):
    i = inputs
    x = np.ascontiguousarray(i["L0_attn_x"], dtype=np.float32)
    B, T, _ = x.shape
    c32 = host_consts(32)
    cores = list(range(8))
    nc = _get_nc("fused", lambda: build_fused(T=T))
    maps = []
    for c in cores:
        b, r = c // 2, c % 2
        sl = slice(r * 512, (r + 1) * 512)
        slh = slice(r * 32, (r + 1) * 32)
        eperm = [4 * r, 4 * r + 1, 4 * r + 2, 4 * r + 3, 4 * (1 - r), 4 * (1 - r) + 1,
                 4 * (1 - r) + 2, 4 * (1 - r) + 3]
        esl = eperm[:4]
        m = dict(
            x=x[b],
            qw=np.ascontiguousarray(i["L0_attn_qw"][sl]),
            kw=np.ascontiguousarray(i["L0_attn_kw"][sl]),
            vw=np.ascontiguousarray(i["L0_attn_vw"][sl]),
            iw=np.ascontiguousarray(i["L0_attn_iw"][slh]),
            fw=np.ascontiguousarray(i["L0_attn_fw"][slh]),
            qb=np.ascontiguousarray(i["L0_attn_qb"][None, sl]),
            kb=np.ascontiguousarray(i["L0_attn_kb"][None, sl]),
            vb=np.ascontiguousarray(i["L0_attn_vb"][None, sl]),
            ib=np.ascontiguousarray(i["L0_attn_ib"][None, slh]),
            fb=np.ascontiguousarray(i["L0_attn_fb"][None, slh]),
            gamma=np.ascontiguousarray(i["L0_attn_gamma"][None, :]),
            beta=np.ascontiguousarray(i["L0_attn_beta"][None, :]),
            ow=i["L0_attn_ow"], ob=np.ascontiguousarray(i["L0_attn_ob"][None, :]),
            gate_w=np.ascontiguousarray(i["L0_ffn_gate_w"][eperm]),
            w1=np.ascontiguousarray(i["L0_ffn_w1"][esl]),
            w3=np.ascontiguousarray(i["L0_ffn_w3"][esl]),
            w2=np.ascontiguousarray(i["L0_ffn_w2"][esl]),
            Wq=np.ascontiguousarray(i["L1_attn_Wq"][:, r * 128:(r + 1) * 128]),
            Wk=np.ascontiguousarray(i["L1_attn_Wk"][:, r * 128:(r + 1) * 128]),
            Wv=np.ascontiguousarray(i["L1_attn_Wv"][:, r * 512:(r + 1) * 512]),
            Wo=i["L1_attn_Wo"],
            Wg=np.ascontiguousarray(i["L1_ffn_Wg"][:, eperm]),
            bg=np.ascontiguousarray(i["L1_ffn_bg"][None, eperm]),
            W0=np.ascontiguousarray(i["L1_ffn_W0"][esl]),
            b0=np.ascontiguousarray(i["L1_ffn_b0"][esl]),
            W1=np.ascontiguousarray(i["L1_ffn_W1"][esl]),
            b1=np.ascontiguousarray(i["L1_ffn_b1"][esl]),
            W2=np.ascontiguousarray(i["L1_ffn_W2"][esl]),
            b2=np.ascontiguousarray(i["L1_ffn_b2"][esl]),
            **c32)
        maps.append({k: v for k, v in m.items() if k in nc.input_names})
    res = run_bass_kernel_spmd(nc, maps, core_ids=cores, trace_cores=[0]).results
    out = np.empty((B, T, D), dtype=np.float32)
    for b in range(B):
        out[b] = np.asarray(res[2 * b]["x4"])
    return out


_NC_CACHE = {}


def _get_nc(name, builder):
    if name not in _NC_CACHE:
        _NC_CACHE[name] = builder()
    return _NC_CACHE[name]


def kernel(**inputs):
    return kernel_fused(**inputs)


def kernel_unfused(**inputs):
    i = inputs
    x = np.ascontiguousarray(i["L0_attn_x"], dtype=np.float32)
    B, T, _ = x.shape
    TK = T // 2
    c32 = host_consts(32)
    cores = list(range(8))

    nc1 = _get_nc("m0", lambda: build_mixer0(T=T, NH=32))
    maps = []
    for c in cores:
        b, hg = c // 2, c % 2
        sl = slice(hg * 512, (hg + 1) * 512)
        slh = slice(hg * 32, (hg + 1) * 32)
        m = dict(
            x=x[b],
            qw=np.ascontiguousarray(i["L0_attn_qw"][sl]),
            kw=np.ascontiguousarray(i["L0_attn_kw"][sl]),
            vw=np.ascontiguousarray(i["L0_attn_vw"][sl]),
            iw=np.ascontiguousarray(i["L0_attn_iw"][slh]),
            fw=np.ascontiguousarray(i["L0_attn_fw"][slh]),
            qb=np.ascontiguousarray(i["L0_attn_qb"][None, sl]),
            kb=np.ascontiguousarray(i["L0_attn_kb"][None, sl]),
            vb=np.ascontiguousarray(i["L0_attn_vb"][None, sl]),
            ib=np.ascontiguousarray(i["L0_attn_ib"][None, slh]),
            fb=np.ascontiguousarray(i["L0_attn_fb"][None, slh]),
            gamma=np.ascontiguousarray(i["L0_attn_gamma"][None, :]),
            beta=np.ascontiguousarray(i["L0_attn_beta"][None, :]),
            **c32)
        maps.append({k: v for k, v in m.items() if k in nc1.input_names})
    r1 = run_bass_kernel_spmd(nc1, maps, core_ids=cores).results
    h0fm = [np.concatenate([np.asarray(r1[2 * b]["hout"]), np.asarray(r1[2 * b + 1]["hout"])], axis=0)
            for b in range(B)]

    nc2 = _get_nc("pf0", lambda: build_postffn0(TK=TK))
    maps = []
    for c in cores:
        b, hf = c // 2, c % 2
        ts = slice(hf * TK, (hf + 1) * TK)
        m = dict(
            x=np.ascontiguousarray(x[b, ts]),
            h0fm=np.ascontiguousarray(h0fm[b][:, ts]),
            ow=i["L0_attn_ow"], ob=np.ascontiguousarray(i["L0_attn_ob"][None, :]),
            gate_w=i["L0_ffn_gate_w"],
            w1=i["L0_ffn_w1"], w3=i["L0_ffn_w3"], w2=i["L0_ffn_w2"],
            **c32)
        maps.append({k: v for k, v in m.items() if k in nc2.input_names})
    r2 = run_bass_kernel_spmd(nc2, maps, core_ids=cores).results
    x2 = [np.concatenate([np.asarray(r2[2 * b]["x2"]), np.asarray(r2[2 * b + 1]["x2"])], axis=0)
          for b in range(B)]

    nc3 = _get_nc("m1", lambda: build_mixer1(T=T, NH=8))
    maps = []
    for c in cores:
        b, hh = c // 2, c % 2
        m = dict(
            x2=x2[b],
            Wq=np.ascontiguousarray(i["L1_attn_Wq"][:, hh * 128:(hh + 1) * 128]),
            Wk=np.ascontiguousarray(i["L1_attn_Wk"][:, hh * 128:(hh + 1) * 128]),
            Wv=np.ascontiguousarray(i["L1_attn_Wv"][:, hh * 512:(hh + 1) * 512]),
            **c32)
        maps.append({k: v for k, v in m.items() if k in nc3.input_names})
    r3 = run_bass_kernel_spmd(nc3, maps, core_ids=cores).results
    yfm = [np.concatenate([np.asarray(r3[2 * b]["yfm"]), np.asarray(r3[2 * b + 1]["yfm"])], axis=0)
           for b in range(B)]

    nc4 = _get_nc("pf1", lambda: build_postffn1(TK=TK))
    maps = []
    for c in cores:
        b, hf = c // 2, c % 2
        ts = slice(hf * TK, (hf + 1) * TK)
        m = dict(
            x2=np.ascontiguousarray(x2[b][ts]),
            yfm=np.ascontiguousarray(yfm[b][:, ts]),
            Wo=i["L1_attn_Wo"], Wg=i["L1_ffn_Wg"], bg=np.ascontiguousarray(i["L1_ffn_bg"][None, :]),
            W0=i["L1_ffn_W0"], b0=i["L1_ffn_b0"],
            W1=i["L1_ffn_W1"], b1=i["L1_ffn_b1"],
            W2=i["L1_ffn_W2"], b2=i["L1_ffn_b2"],
            **c32)
        maps.append({k: v for k, v in m.items() if k in nc4.input_names})
    r4 = run_bass_kernel_spmd(nc4, maps, core_ids=cores).results
    out = np.empty((B, T, D), dtype=np.float32)
    for c in cores:
        b, hf = c // 2, c % 2
        out[b, hf * TK:(hf + 1) * TK] = np.asarray(r4[c]["x4"])
    return out
